# version 19
# baseline (speedup 1.0000x reference)
import numpy as np
import ml_dtypes
from contextlib import ExitStack

import concourse.bass as bass
import concourse.mybir as mybir
from concourse.bass_utils import run_bass_kernel_spmd

F32 = mybir.dt.float32
BF16 = mybir.dt.bfloat16
AX = mybir.AxisListType.X
ALU = mybir.AluOpType
AF = mybir.ActivationFunctionType
bfdt = ml_dtypes.bfloat16

B, S, E, H, DH, FF = 4, 4096, 2048, 16, 128, 8192
TQ = 2048
SCALE = float(1.0 / np.sqrt(np.float64(128.0)))
CLIPB = float(np.float32(128.0 - 1e-6))
GUARD = 1e-30

ENGS = ("sync", "gpsimd", "vector", "scalar", "tensor")


class Bld:
    def __init__(self, nc):
        self.nc = nc
        self.q = {k: [] for k in ENGS}
        self.stack = ExitStack()
        self.sems = {}
        self.cnt = {}

    def on(self, eng, fn):
        self.q[eng].append(fn)

    def sem(self, name):
        if name not in self.sems:
            self.sems[name] = self.stack.enter_context(self.nc.semaphore(name))
            self.cnt[name] = 0
        return self.sems[name]

    def inc(self, name, k=1):
        self.sem(name)
        self.cnt[name] += k
        return self.cnt[name]

    def c(self, name):
        return self.cnt.get(name, 0)

    def wait(self, eng, name, v):
        if v > 0:
            h = self.sem(name)
            self.on(eng, lambda e, h=h, v=v: e.wait_ge(h, v))

    def barrier(self):
        snap = [(self.sems[n], self.cnt[n]) for n in self.sems if self.cnt[n] > 0]
        for eng in ENGS:
            for h, v in snap:
                self.on(eng, lambda e, h=h, v=v: e.wait_ge(h, v))

    def sb(self, name, shape, dtype, stk=None):
        return (stk or self.stack).enter_context(self.nc.sbuf_tensor(name, shape, dtype))

    def dma(self, eng, name, dst, src):
        if not hasattr(self, "ring_idx"):
            self.ring_idx = {}
        nring = {"sTR": 4, "sOU": 4, "sAR": 4, "sHR": 4, "sPJ": 4}.get(name, 2)
        i = self.ring_idx[name] = (self.ring_idx.get(name, -1) + 1) % nring
        sub = f"{name}~{i}"
        h = self.sem(sub)
        self.on(eng, lambda e, d=dst, s=src, h=h: e.dma_start(out=d, in_=s).then_inc(h, 16))
        return (sub, self.inc(sub, 16))

    def dma_group(self, eng, name, pairs, nring=8):
        if not hasattr(self, "ring_idx"):
            self.ring_idx = {}
        i = self.ring_idx[name] = (self.ring_idx.get(name, -1) + 1) % nring
        sub = f"{name}~{i}"
        h = self.sem(sub)
        for dst, src in pairs:
            self.on(eng, lambda e, d=dst, s=src, h=h: e.dma_start(out=d, in_=s).then_inc(h, 16))
            self.inc(sub, 16)
        return (sub, self.c(sub))

    def waitv(self, eng, tok):
        if tok:
            self.wait(eng, tok[0], tok[1])

    def wait_all(self, eng, name):
        for i in range(8):
            sub = f"{name}~{i}"
            if sub in self.sems and self.cnt[sub] > 0:
                self.wait(eng, sub, self.cnt[sub])

    def vops(self, ops, eng="vector", sem=None):
        sem = sem or ("vX" if eng == "vector" else "aX")
        h = self.sem(sem)
        for fn in ops:
            self.wait(eng, sem, self.c(sem))
            self.on(eng, lambda e, fn=fn, h=h: fn(e).then_inc(h, 1))
            self.inc(sem)
        return self.c(sem)


def build_nc(debug=False):
    nc = bass.Bass("TRN2", target_bir_lowering=False)
    b = Bld(nc)

    xf = nc.dram_tensor("xf", [S, E], F32, kind="ExternalInput").ap()
    xq = nc.dram_tensor("xq", [TQ, E], F32, kind="ExternalInput").ap()
    wqT = nc.dram_tensor("wqT", [E, E], F32, kind="ExternalInput").ap()
    wkT = nc.dram_tensor("wkT", [E, E], F32, kind="ExternalInput").ap()
    wvT = nc.dram_tensor("wvT", [E, E], F32, kind="ExternalInput").ap()
    woT = nc.dram_tensor("woT", [E, E], F32, kind="ExternalInput").ap()
    wfcT = nc.dram_tensor("wfcT", [E, FF], F32, kind="ExternalInput").ap()
    wpjT = nc.dram_tensor("wpjT", [FF, E], F32, kind="ExternalInput").ap()
    bo = nc.dram_tensor("bo", [1, E], F32, kind="ExternalInput").ap()
    ptm = nc.dram_tensor("ptm", [8, 128, 512], BF16, kind="ExternalInput").ap()
    idn = nc.dram_tensor("idn", [128, 128], BF16, kind="ExternalInput").ap()
    out = nc.dram_tensor("out", [TQ, E], F32, kind="ExternalOutput").ap()

    ks = "ExternalOutput" if debug else "Internal"
    XNF = nc.dram_tensor("XNF", [16, 128, S], BF16, kind=ks).ap()
    Ks = nc.dram_tensor("Ks", [H, 128, S], BF16, kind=ks).ap()
    Vs = nc.dram_tensor("Vs", [S, H * 129], BF16, kind=ks).ap()
    CTX = nc.dram_tensor("CTX", [H, TQ, DH], BF16, kind=ks).ap()
    AOUT = nc.dram_tensor("AOUT", [TQ, E], F32, kind=ks).ap()
    Hs = nc.dram_tensor("Hs", [TQ, FF], F32, kind=ks).ap()
    TWPJ = nc.dram_tensor("TWPJ", [FF, E], BF16, kind=ks).ap()
    QDBG = nc.dram_tensor("QDBG", [128, H * TQ], BF16, kind=ks).ap() if debug else None
    GDBG = nc.dram_tensor("GDBG", [128, 16], F32, kind=ks).ap() if debug else None

    with b.stack:
        pk = b.stack.enter_context(nc.psum_tensor("pk", [128, 1024], F32))
        ptr0 = b.stack.enter_context(nc.psum_tensor("ptr0", [128, 1024], BF16))
        ptr1 = b.stack.enter_context(nc.psum_tensor("ptr1", [128, 1024], BF16))
        pv = b.stack.enter_context(nc.psum_tensor("pv", [128, 2048], F32))
        ptrs = (ptr0, ptr1)
        pkh = (pk[:, 0:512], pk[:, 512:1024])
        pbf = (ptr0.bitcast(F32).ap(), ptr1.bitcast(F32).ap())

        ident = b.sb("ident", [128, 128], BF16)
        ones_bf = b.sb("ones_bf", [128, 1], BF16)
        onesr_bf = b.sb("onesr_bf", [1, 128], BF16)
        ones_f32 = b.sb("ones_f32", [128, 1], F32)
        stats = b.sb("stats", [128, 64], F32)
        stats2 = b.sb("stats2", [128, 64], F32)
        sc0 = b.sb("sc0", [1, 64], F32)
        kmax2 = b.sb("kmax2", [1, 16], F32)
        qmax2 = b.sb("qmax2", [1, 64], F32)
        t64 = b.sb("t64", [1, 64], F32)
        tall = b.sb("tall", [1, 64], BF16)
        bias_all = b.sb("bias_all", [128, 64], F32)
        gacc = b.sb("gacc", [128, 2], F32)
        gcols = b.sb("gcols", [128, 512], F32)
        gall = b.sb("gall", [128, 6], F32)
        gfc_b = gall[:, 0:1]
        gpj_b = gall[:, 1:2]
        thfc_b = gall[:, 2:3]
        thpj_b = gall[:, 3:4]
        nthfc_b = gall[:, 4:5]
        nthpj_b = gall[:, 5:6]
        onesr_f32 = b.sb("onesr_f32", [1, 128], F32)
        tmp1 = b.sb("tmp1", [128, 32], F32)
        bnst = b.sb("bnst", [128, 96], F32)
        bnag = b.sb("bnag", [128, 8], F32)

        idn_tok = b.dma("sync", "sIN", ident[:], idn)
        b.on("gpsimd", lambda e: e.memset(ones_bf[:], 1.0))
        b.on("gpsimd", lambda e: e.memset(onesr_bf[:], 1.0))
        b.on("gpsimd", lambda e: e.memset(ones_f32[:], 1.0))
        b.on("gpsimd", lambda e: e.memset(onesr_f32[:], 1.0))
        b.on("gpsimd", lambda e: e.memset(gacc[:], 0.0))
        b.on("gpsimd", lambda e: e.memset(kmax2[:], 0.0).then_inc(b.sem("gA"), 1))
        b.inc("gA")

        gchunks = []
        for (wT, col, sname) in ((wpjT, 1, "sGP"), (wfcT, 0, "sGF")):
            wv_ = wT.rearrange("(n p) c -> n p c", p=128)
            for it in range(wv_.shape[0]):
                for ch in range(wv_.shape[2] // 2048):
                    gchunks.append((wv_[it, :, 2048 * ch:2048 * (ch + 1)], col, sname))
        gk = [0]
        gtok3 = [None, None, None]
        g_act = [0, 0, 0]

        def g_issue(kk):
            if kk >= len(gchunks):
                return
            src, col, sname = gchunks[kk]
            buf = gbb[kk % 3]
            if kk >= 3:
                b.wait("gpsimd", "aG", g_act[kk % 3])
            gtok3[kk % 3] = b.dma("gpsimd", sname, buf[:], src)

        def emit_gamma_chunks(n):
            for _ in range(n):
                kk = gk[0]
                if kk >= len(gchunks):
                    return
                if kk == 0:
                    for j in range(2):
                        g_issue(j)
                buf = gbb[kk % 3]
                b.waitv("gpsimd", gtok3[kk % 3])
                gh = b.sem("gG")
                b.wait("gpsimd", "gG", b.c("gG"))
                b.on("gpsimd", lambda e, h=gh: e.memset(stats2[0:1, 2:3], 0.0).then_inc(h, 1))
                sig = b.inc("gG")
                b.wait("scalar", "gG", sig)
                agh = b.sem("aG")
                for j in range(4):
                    cc4 = 4 * kk + j
                    if j == 3:
                        b.on("scalar", lambda e, bu=buf, j=j, c=cc4, h=agh: e.activation(bu[:, 512 * j:512 * (j + 1)], bu[:, 512 * j:512 * (j + 1)], AF.Abs, accum_out=gcols[:, c:c + 1]).then_inc(h, 1))
                    else:
                        b.on("scalar", lambda e, bu=buf, j=j, c=cc4: e.activation(bu[:, 512 * j:512 * (j + 1)], bu[:, 512 * j:512 * (j + 1)], AF.Abs, accum_out=gcols[:, c:c + 1]))
                g_act[kk % 3] = b.inc("aG")
                g_issue(kk + 2)
                gk[0] += 1

        tr_war = [0, 0]
        xnt_war = [0]

        def ln_tile(xsrc, xbuf, xnt, xscr, lsem, qti, xnf_dst_fn,
                    xbuf_war, evac_waits):
            raise NotImplementedError

        bn_war = [0, 0]

        def ln_group(srcs, qtis, xts, xnt, lsem, xw, xnf_dst_fns, first_evac_waits, per_tile_hook=None):
            nT = len(srcs)

            def stageA(i):
                pp = i % 2
                B = 16 * pp
                b.wait("sync", "vX", xw[pp])
                lv = b.dma("sync", lsem, xts[pp][:], srcs[i])
                b.waitv("vector", lv)
                b.wait("vector", "vX", bn_war[pp])
                vxh = b.sem("vX")
                for g in range(7):
                    b.on("vector", lambda e, bu=xts[pp], g=g, pp=pp: e.bn_stats(bnst[:, 48 * pp + 6 * g:48 * pp + 6 * g + 6], bu[:, 256 * g:256 * (g + 1)]))
                b.on("vector", lambda e, bu=xts[pp], pp=pp, h=vxh: e.bn_stats(bnst[:, 48 * pp + 42:48 * pp + 48], bu[:, 1792:2048]).then_inc(h, 1))
                b.inc("vX")
                is_q = qtis[i] is not None
                ops = [lambda e, pp=pp: e.bn_aggr(bnag[:, 4 * pp:4 * pp + 2], bnst[:, 48 * pp:48 * pp + 48]),
                       lambda e, pp=pp, B=B: e.tensor_scalar(tmp1[:, B:B + 1], bnag[:, 4 * pp + 1:4 * pp + 2], 1e-5, None, ALU.add)]
                if is_q:
                    ops.append(lambda e, pp=pp, B=B: e.tensor_scalar(tmp1[:, B + 1:B + 2], bnag[:, 4 * pp + 1:4 * pp + 2], 1.0 + 1e-6, None, ALU.mult))
                    ops.append(lambda e, B=B: e.tensor_scalar(tmp1[:, B + 1:B + 2], tmp1[:, B + 1:B + 2], 1e-5 * 1e-6, None, ALU.add))
                ops.append(lambda e, pp=pp, B=B: e.tensor_copy(tmp1[:, B + 2:B + 3], bnag[:, 4 * pp:4 * pp + 1]))
                den_v = b.vops(ops)
                bn_war[pp] = den_v
                return den_v

            def stageB(i, den_v):
                B = 16 * (i % 2)
                ncol = 2 if qtis[i] is not None else 1
                b.wait("scalar", "vX", den_v)
                return b.vops([
                    lambda e, n=ncol, B=B: e.activation(tmp1[:, B + 4:B + 4 + n], tmp1[:, B:B + n], AF.Ln),
                    lambda e, n=ncol, B=B: e.activation(tmp1[:, B + 6:B + 6 + n], tmp1[:, B + 4:B + 4 + n], AF.Exp, scale=-0.5),
                ], eng="scalar")

            def stageC(i, rstd_v):
                pp = i % 2
                B = 16 * pp
                xbuf = xts[pp]
                b.wait("vector", "aX", rstd_v)
                b.wait("vector", "tT", xnt_war[0])
                if i == 0:
                    for (sname, v) in first_evac_waits:
                        b.wait("vector", sname, v)
                xn_v = b.vops([lambda e, bu=xbuf, B=B: e.tensor_scalar(xnt[:], bu[:], tmp1[:, B + 2:B + 3], tmp1[:, B + 6:B + 7], ALU.subtract, ALU.mult)])
                last_xbuf_v = xn_v
                if qtis[i] is not None:
                    cc = 4 * qtis[i]
                    b.vops([
                        lambda e, bu=xbuf, B=B: e.tensor_reduce(tmp1[:, B + 10:B + 11], bu[:], AX, ALU.max),
                        lambda e, bu=xbuf, B=B: e.tensor_reduce(tmp1[:, B + 11:B + 12], bu[:], AX, ALU.min),
                    ])
                    last_xbuf_v = b.c("vX")
                    b.vops([
                        lambda e, B=B: e.tensor_tensor(tmp1[:, B + 10:B + 11], tmp1[:, B + 10:B + 11], tmp1[:, B + 2:B + 3], ALU.subtract),
                        lambda e, B=B: e.tensor_tensor(tmp1[:, B + 11:B + 12], tmp1[:, B + 2:B + 3], tmp1[:, B + 11:B + 12], ALU.subtract),
                        lambda e, B=B: e.tensor_tensor(tmp1[:, B + 3:B + 4], tmp1[:, B + 10:B + 11], tmp1[:, B + 11:B + 12], ALU.max),
                        lambda e, c=cc, B=B: e.tensor_copy(stats[:, c:c + 1], tmp1[:, B + 2:B + 3]),
                        lambda e, c=cc, B=B: e.scalar_tensor_tensor(stats[:, c + 2:c + 3], tmp1[:, B + 3:B + 4], 1.0 / 128.0, tmp1[:, B + 7:B + 8], ALU.mult, ALU.mult),
                        lambda e, B=B: e.reciprocal(tmp1[:, B + 5:B + 6], tmp1[:, B + 3:B + 4]),
                        lambda e, c=cc, B=B: e.tensor_scalar(stats[:, c + 1:c + 2], tmp1[:, B + 5:B + 6], 128.0, None, ALU.mult),
                    ])
                xw[pp] = last_xbuf_v
                b.wait("tensor", "vX", xn_v)
                tth = b.sem("tT")
                vbh = b.sem("vB")
                for jh in range(2):
                    pt = ptrs[jh]
                    b.wait("tensor", "vB", tr_war[jh])
                    for j in range(8):
                        jj = jh * 8 + j
                        if j == 7:
                            b.on("tensor", lambda e, pt=pt, j=j, jj=jj, xnt=xnt, h=tth: e.transpose(pt[:, 128 * j:128 * (j + 1)], xnt[:, 128 * jj:128 * (jj + 1)], ident[:]).then_inc(h, 1))
                        else:
                            b.on("tensor", lambda e, pt=pt, j=j, jj=jj, xnt=xnt: e.transpose(pt[:, 128 * j:128 * (j + 1)], xnt[:, 128 * jj:128 * (jj + 1)], ident[:]))
                    tv = b.inc("tT")
                    b.wait("vector", "tT", tv)
                    dst = xnf_dst_fns[i](jh)
                    b.on("vector", lambda e, pt=pt, d=dst, h=vbh: e.tensor_copy(d, pt[:].rearrange("p (j c) -> p j c", c=128)).then_inc(h, 1))
                    tr_war[jh] = b.inc("vB")
                xnt_war[0] = b.c("tT")

            rst = {}
            for i in range(nT):
                den = stageA(i)
                if i >= 1:
                    stageC(i - 1, rst[i - 1])
                rst[i] = stageB(i, den)
                if per_tile_hook:
                    per_tile_hook()
            stageC(nT - 1, rst[nT - 1])

        p1s = ExitStack()
        gbb = [b.sb(f"gbb{i}", [128, 2048], F32, p1s) for i in range(3)]
        wv_b = b.sb("wv_b", [128, 16 * E], BF16, p1s)
        xt1a = b.sb("xt1a", [128, E], F32, p1s)
        xt1b = b.sb("xt1b", [128, E], F32, p1s)
        xnt1 = b.sb("xnt1", [128, E], BF16, p1s)
        xnfg1 = b.sb("xnfg1", [128, 16 * 1024], BF16, p1s)
        vst = [b.sb("vst0", [128, H * 129], BF16, p1s), b.sb("vst1", [128, H * 129], BF16, p1s)]
        kst = [b.sb("kst0", [128, 512], BF16, p1s), b.sb("kst1", [128, 512], BF16, p1s)]
        ksq = b.sb("ksq", [128, 512], BF16, p1s)
        xts = [xt1a, xt1b]

        for v_ in vst:
            vc = v_[:].rearrange("p (h c) -> p h c", c=129)[:, :, 128:129]
            b.on("gpsimd", lambda e, d=vc: e.memset(d, 1.0).then_inc(b.sem("gA"), 1))
            b.inc("gA")
        gA_vst = b.c("gA")

        wvv = wvT.rearrange("(dc p) o -> p dc o", p=128)
        wv_dst = wv_b[:].rearrange("p (dc o) -> p dc o", o=E)
        wv_rdy = b.dma("gpsimd", "sW", wv_dst, wvv)

        xw = [0, 0]
        pv_war = [0, 0]
        vst_war = [None, None]
        xnf_toks = [None, None]
        b.waitv("tensor", idn_tok)
        for g in range(4):
            def mk_dst1(t8):
                return lambda jh: xnfg1[:].rearrange("p (j c) -> p j c", c=1024)[:, 8 * jh:8 * jh + 8, 128 * t8:128 * (t8 + 1)]
            ln_group([xf[128 * (g * 8 + t8):128 * (g * 8 + t8 + 1), :] for t8 in range(8)],
                     [None] * 8, xts, xnt1, "sX1", xw,
                     [mk_dst1(t8) for t8 in range(8)],
                     [("tC", b.c("tC"))] + [t for t in xnf_toks if t],
                     per_tile_hook=lambda: emit_gamma_chunks(4))
            b.wait("sync", "vB", b.c("vB"))
            for j in range(16):
                xnf_toks[j % 2] = b.dma("sync", "sXN", XNF[j, :, 1024 * g:1024 * (g + 1)], xnfg1[:, 1024 * j:1024 * (j + 1)])
            b.waitv("tensor", wv_rdy)
            b.wait("tensor", "gA", gA_vst)
            b.wait("tensor", "vB", b.c("vB"))
            for t8 in range(8):
                ti = g * 8 + t8
                vstage = vst[ti % 2]
                tch = b.sem("tC")
                vch = b.sem("vC")
                for vh in range(2):
                    b.wait("tensor", "vC", pv_war[vh])
                    for s2 in range(2):
                        for d in range(16):
                            lhs = xnfg1[:, 1024 * d + 128 * t8: 1024 * d + 128 * (t8 + 1)]
                            rhs = wv_b[:, E * d + 1024 * vh + 512 * s2: E * d + 1024 * vh + 512 * (s2 + 1)]
                            pslc = pv[:, 1024 * vh + 512 * s2: 1024 * vh + 512 * (s2 + 1)]
                            if d == 15 and s2 == 1:
                                b.on("tensor", lambda e, p=pslc, l=lhs, r=rhs, h=tch: e.matmul(p, l, r, start=False, stop=True).then_inc(h, 1))
                            else:
                                b.on("tensor", lambda e, p=pslc, l=lhs, r=rhs, d=d: e.matmul(p, l, r, start=(d == 0), stop=(d == 15)))
                    tv = b.inc("tC")
                    b.wait("vector", "tC", tv)
                    b.waitv("vector", vst_war[ti % 2])
                    dstv = vstage[:].rearrange("p (h c) -> p h c", c=129)[:, 8 * vh:8 * vh + 8, 0:128]
                    srcv = pv[:, 1024 * vh:1024 * (vh + 1)].rearrange("p (h c) -> p h c", c=128)
                    b.on("vector", lambda e, d=dstv, s=srcv, h=vch: e.tensor_copy(d, s).then_inc(h, 1))
                    pv_war[vh] = b.inc("vC")
                b.wait("sync", "vC", b.c("vC"))
                vst_war[ti % 2] = b.dma("sync", "sVS", Vs[128 * ti:128 * (ti + 1), :], vstage[:])

        wk_b = wv_b
        b.wait("gpsimd", "tC", b.c("tC"))
        wkk = wkT.rearrange("(dc p) o -> p dc o", p=128)
        wk_rdy = b.dma("gpsimd", "sW", wv_dst, wkk)
        kwar = [0, 0]
        kdma_war = [None, None]
        for g in range(4):
            b.wait("sync", "tC", b.c("tC"))
            b.wait("sync", "tD", b.c("tD"))
            b.wait_all("sync", "sXN")
            for j in range(16):
                b.dma("sync", "sXR", xnfg1[:, 1024 * j:1024 * (j + 1)], XNF[j, :, 1024 * g:1024 * (g + 1)])
            b.wait_all("tensor", "sXR")
            if g == 0:
                b.waitv("tensor", wk_rdy)
            for h_ in range(H):
                for t5 in range(2):
                    idx = g * 32 + h_ * 2 + t5
                    kb = kst[idx % 2]
                    ph = idx % 2
                    b.wait("tensor", "vC", kwar[ph])
                    tdh = b.sem("tD")
                    for d in range(16):
                        lhs = wk_b[:, E * d + 128 * h_: E * d + 128 * (h_ + 1)]
                        rhs = xnfg1[:, 1024 * d + 512 * t5: 1024 * d + 512 * (t5 + 1)]
                        if d == 15:
                            b.on("tensor", lambda e, l=lhs, r=rhs, p=ph, h=tdh: e.matmul(pkh[p], l, r, start=False, stop=True).then_inc(h, 1))
                        else:
                            b.on("tensor", lambda e, l=lhs, r=rhs, p=ph, d=d: e.matmul(pkh[p], l, r, start=(d == 0), stop=False))
                    tv = b.inc("tD")
                    b.wait("vector", "tD", tv)
                    b.waitv("vector", kdma_war[idx % 2])
                    vch = b.sem("vC")
                    b.on("vector", lambda e, kb=kb, p=ph, h=vch: e.tensor_copy(kb[:], pkh[p]).then_inc(h, 1))
                    kev = b.inc("vC")
                    kwar[ph] = kev
                    b.on("vector", lambda e, kb=kb, h=vch: e.tensor_tensor(ksq[:], kb[:], kb[:], ALU.mult).then_inc(h, 1))
                    ksq_v = b.inc("vC")
                    b.wait("tensor", "vC", ksq_v)
                    knh = b.sem("tA")
                    b.on("tensor", lambda e, h=knh: e.matmul(pv[0:1, 0:512], ones_bf[:], ksq[:], start=True, stop=True).then_inc(h, 1))
                    kn_t = b.inc("tA")
                    b.wait("vector", "tA", kn_t)
                    kmv = b.vops([
                        lambda e: e.tensor_reduce(tmp1[0:1, 9:10], pv[0:1, 0:512], AX, ALU.max),
                        lambda e, hh=h_: e.tensor_tensor(kmax2[:, hh:hh + 1], kmax2[:, hh:hh + 1], tmp1[0:1, 9:10], ALU.max),
                    ])
                    b.wait("sync", "vC", kev)
                    kdma_war[idx % 2] = b.dma("sync", "sKS", Ks[h_, :, 1024 * g + 512 * t5: 1024 * g + 512 * (t5 + 1)], kb[:])
        kmax_done_v = b.c("vX")

        assert gk[0] == len(gchunks), "gamma stream must drain inside P1 (gbb dies with p1s)"
        p1s.close()
        b.barrier()

        p23s = ExitStack()
        q_fm = b.sb("q_fm", [128, H * TQ], BF16, p23s)
        p2s = ExitStack()
        wq_b = b.sb("wq_b", [128, 16 * E], BF16, p2s)
        xt2a = b.sb("xt2a", [128, E], F32, p2s)
        xt2b = b.sb("xt2b", [128, E], F32, p2s)
        xnt2 = b.sb("xnt2", [128, E], BF16, p2s)
        xnfg2 = b.sb("xnfg2", [128, 16 * 1024], BF16, p2s)
        qsq = b.sb("qsq", [128, 2048], BF16, p2s)
        xts2 = [xt2a, xt2b]

        wqq = wqT.rearrange("(dc p) o -> p dc o", p=128)
        wq_rdy = b.dma("gpsimd", "sW", wq_b[:].rearrange("p (dc o) -> p dc o", o=E), wqq)

        xw2 = [0, 0]
        qwar = [0, 0]
        for g2 in range(2):
            def mk_dst2(t8):
                return lambda jh: xnfg2[:].rearrange("p (j c) -> p j c", c=1024)[:, 8 * jh:8 * jh + 8, 128 * t8:128 * (t8 + 1)]
            ln_group([xq[128 * (g2 * 8 + t8):128 * (g2 * 8 + t8 + 1), :] for t8 in range(8)],
                     [g2 * 8 + t8 for t8 in range(8)], xts2, xnt2, "sQX", xw2,
                     [mk_dst2(t8) for t8 in range(8)],
                     [("tD", b.c("tD"))], per_tile_hook=lambda: emit_gamma_chunks(2))
            b.wait("tensor", "vB", b.c("vB"))
            if g2 == 0:
                b.waitv("tensor", wq_rdy)
            for h_ in range(H):
                for t5 in range(2):
                    idx = g2 * 32 + h_ * 2 + t5
                    ph = idx % 2
                    b.wait("tensor", "vC", qwar[ph])
                    tdh = b.sem("tD")
                    for d in range(16):
                        lhs = wq_b[:, E * d + 128 * h_: E * d + 128 * (h_ + 1)]
                        rhs = xnfg2[:, 1024 * d + 512 * t5: 1024 * d + 512 * (t5 + 1)]
                        if d == 15:
                            b.on("tensor", lambda e, l=lhs, r=rhs, p=ph, h=tdh: e.matmul(pkh[p], l, r, start=False, stop=True).then_inc(h, 1))
                        else:
                            b.on("tensor", lambda e, l=lhs, r=rhs, p=ph, d=d: e.matmul(pkh[p], l, r, start=(d == 0), stop=False))
                    tv = b.inc("tD")
                    b.wait("vector", "tD", tv)
                    dstq = q_fm[:, TQ * h_ + 1024 * g2 + 512 * t5: TQ * h_ + 1024 * g2 + 512 * (t5 + 1)]
                    vch = b.sem("vC")
                    b.on("vector", lambda e, d=dstq, p=ph, h=vch: e.tensor_copy(d, pkh[p]).then_inc(h, 1))
                    qwar[ph] = b.inc("vC")
        emit_gamma_chunks(len(gchunks))

        b.wait("tensor", "vC", b.c("vC"))
        for h_ in range(H):
            b.wait("vector", "tA", b.c("tA"))
            vch = b.sem("vC")
            b.on("vector", lambda e, h_=h_, h=vch: e.tensor_tensor(qsq[:], q_fm[:, TQ * h_:TQ * h_ + 2048], q_fm[:, TQ * h_:TQ * h_ + 2048], ALU.mult).then_inc(h, 1))
            qv = b.inc("vC")
            b.wait("tensor", "vC", qv)
            for g in range(4):
                b.wait("tensor", "vA", b.c("vA"))
                tah = b.sem("tA")
                b.on("tensor", lambda e, g=g, h=tah: e.matmul(pv[0:1, 0:512], ones_bf[:], qsq[:, 512 * g:512 * (g + 1)], start=True, stop=True).then_inc(h, 1))
                tv = b.inc("tA")
                b.wait("vector", "tA", tv)
                col = 4 * h_ + g
                vah = b.sem("vA")
                b.on("vector", lambda e, c=col, h=vah: e.tensor_reduce(qmax2[0:1, c:c + 1], pv[0:1, 0:512], AX, ALU.max).then_inc(h, 1))
                b.inc("vA")
        b.wait("vector", "vA", b.c("vA"))
        b.wait("vector", "vX", kmax_done_v)
        tops = [(lambda e, h_=h_: e.tensor_scalar(t64[0:1, 4 * h_:4 * h_ + 4], qmax2[0:1, 4 * h_:4 * h_ + 4], kmax2[0:1, h_:h_ + 1], None, ALU.add)) for h_ in range(H)]
        tops.append(lambda e: e.tensor_scalar(tall[:], t64[:], -0.5 * SCALE, None, ALU.mult))
        tall_v = b.vops(tops)
        b.wait("tensor", "vX", tall_v)
        tah = b.sem("tA")
        b.on("tensor", lambda e, h=tah: e.matmul(pv[0:128, 512:576], onesr_bf[:], tall[:], start=True, stop=True).then_inc(h, 1))
        tv = b.inc("tA")
        b.wait("scalar", "tA", tv)
        abh = b.sem("aB")
        b.on("scalar", lambda e, h=abh: e.activation(bias_all[:], pv[0:128, 512:576], AF.Copy).then_inc(h, 1))
        bias_rdy = b.inc("aB")

        b.wait("vector", "aG", b.c("aG"))
        b.vops([
            lambda e: e.tensor_reduce(gacc[:, 1:2], gcols[:, 0:256], AX, ALU.add),
            lambda e: e.tensor_reduce(gacc[:, 0:1], gcols[:, 256:512], AX, ALU.add),
        ])
        b.wait("tensor", "vX", b.c("vX"))
        b.wait("tensor", "vA", b.c("vA"))
        tah = b.sem("tA")
        b.on("tensor", lambda e, h=tah: e.matmul(pv[0:1, 0:2], ones_f32[:], gacc[:], start=True, stop=True).then_inc(h, 1))
        gm_v = b.inc("tA")
        b.wait("scalar", "tA", gm_v)
        abh = b.sem("aB")
        b.on("scalar", lambda e, h=abh: e.activation(sc0[:, 0:2], pv[0:1, 0:2], AF.Copy).then_inc(h, 1))
        b.inc("aB")
        b.wait("vector", "aB", b.c("aB"))
        gfin = b.vops([
            lambda e: e.tensor_scalar(sc0[:, 2:4], sc0[:, 0:2], 1.0 / (E * FF), None, ALU.mult),
            lambda e: e.tensor_scalar(sc0[:, 4:6], sc0[:, 2:4], 1e-6, None, ALU.add),
            lambda e: e.tensor_scalar(sc0[:, 4:6], sc0[:, 4:6], 0.5, None, ALU.mult),
            lambda e: e.tensor_scalar(sc0[:, 6:8], sc0[:, 4:6], -1.0, None, ALU.mult),
        ])
        b.wait("tensor", "vX", gfin)
        b.wait("tensor", "aB", b.c("aB"))
        tah2 = b.sem("tA")
        b.on("tensor", lambda e, h=tah2: e.matmul(pv[0:128, 512:518], onesr_f32[:], sc0[:, 2:8], start=True, stop=True).then_inc(h, 1))
        gbt = b.inc("tA")
        b.wait("scalar", "tA", gbt)
        sgbh = b.sem("sGB")
        b.on("scalar", lambda e, h=sgbh: e.activation(gall[:], pv[0:128, 512:518], AF.Copy).then_inc(h, 1))
        gb_rdy = b.inc("sGB")
        if debug:
            b.wait("sync", "vC", b.c("vC"))
            b.dma("sync", "sDB", QDBG, q_fm[:])
            b.wait("sync", "sGB", gb_rdy)
            b.dma("sync", "sDB", GDBG[:, 0:6], gall[:])
            b.dma("sync", "sDB", GDBG[:, 6:8], gacc[:])
            b.dma("sync", "sDB", GDBG[0:1, 8:16], sc0[:, 0:8])

        p2s.close()
        b.barrier()

        p3os = ExitStack()
        p3s = ExitStack()
        kbuf = [b.sb("kb0", [128, S], BF16, p3s), b.sb("kb1", [128, S], BF16, p3s)]
        vbuf = [b.sb("vb0", [128, 32 * 258], BF16, p3s), b.sb("vb1", [128, 32 * 258], BF16, p3s)]
        att = [b.sb(f"att{i}", [128, 512], BF16, p3s) for i in range(4)]
        cst = [b.sb("cst0", [128, 512], BF16, p3s), b.sb("cst1", [128, 512], BF16, p3s)]
        masks = b.sb("masks", [128, 8 * 512], BF16, p3s)
        pjst = [b.sb(f"pjst{i}", [128, 512], F32, p3s) for i in range(3)]
        _gcb = gcols.bitcast(BF16).ap()
        pjtn = [_gcb[:, 0:512], _gcb[:, 512:1024]]
        rlb = b.sb("rlb", [128, 4], F32, p3s)
        rlc = b.sb("rlc", [128, 4], F32, p3s)

        pm = ptm.rearrange("r p c -> p r c")
        mk_rdy = b.dma("sync", "sIN", masks[:].rearrange("p (r c) -> p r c", c=512), pm)

        pjk = [0]
        tw_toks = [None, None]
        wpv = wpjT.rearrange("(n p) c -> n p c", p=128)

        pj_lv = {}
        pj_cv = [0, 0, 0]

        def pj_issue(kk):
            if kk >= 256:
                return
            bufA = pjst[kk % 3]
            if kk >= 3:
                b.wait("gpsimd", "vX", pj_cv[kk % 3])
            src = wpv[kk // 4, :, 512 * (kk % 4):512 * (kk % 4) + 512]
            pj_lv[kk] = [b.dma("gpsimd", "sPJ", bufA[:, 256 * hh:256 * (hh + 1)],
                               src[:, 256 * hh:256 * (hh + 1)]) for hh in range(2)]

        def emit_pj_chunk(n=1):
            for _ in range(n):
                kk = pjk[0]
                if kk >= 256:
                    return
                if kk == 0:
                    b.wait("vector", "sGB", gb_rdy)
                    for j in range(2):
                        pj_issue(j)
                bufA = pjst[kk % 3]
                bufB = pjtn[kk % 2]
                if kk >= 2:
                    b.waitv("vector", tw_toks[kk % 2])
                for t_ in pj_lv[kk]:
                    b.waitv("vector", t_)
                tv = b.vops([
                    lambda e, a=bufA, t=bufB: e.tensor_single_scalar(t, a[:], thpj_b, ALU.is_gt),
                    lambda e, a=bufA, t=bufB: e.scalar_tensor_tensor(t, a[:], nthpj_b, t, ALU.is_ge, ALU.add),
                ])
                pj_cv[kk % 3] = tv
                b.wait("gpsimd", "vX", tv)
                tw_toks[kk % 2] = b.dma("gpsimd", "sTW", TWPJ[128 * (kk // 4):128 * (kk // 4 + 1), 512 * (kk % 4):512 * (kk % 4) + 512], bufB)
                pj_issue(kk + 2)
                pjk[0] += 1

        kh_rdy = [b.dma("sync", "sKH", kbuf[0][:], Ks[0, :, :]),
                  b.dma("sync", "sKH", kbuf[1][:], Ks[1, :, :])]
        vsr = Vs.rearrange("(k p) c -> p k c", p=128)
        vp_rdy = [b.dma("sync", "sVP", vbuf[0][:].rearrange("p (k c) -> p k c", c=258), vsr[:, :, 0:258]),
                  b.dma("sync", "sVP", vbuf[1][:].rearrange("p (k c) -> p k c", c=258), vsr[:, :, 258:516])]

        ctx_slot_war = [0]
        cst_war = [None, None]
        ts_after_head = []
        tx_after_head = []
        for h_ in range(H):
            b.waitv("tensor", kh_rdy[h_])
            b.waitv("tensor", vp_rdy[h_ // 2])
            if h_ == 0:
                b.wait("scalar", "aB", bias_rdy)
                b.waitv("vector", mk_rdy)
                b.waitv("scalar", mk_rdy)
            kb = kbuf[h_ % 2]
            vb_ = vbuf[(h_ // 2) % 2]
            vslice = vb_[:].rearrange("p (k c) -> p k c", c=258)
            voff = 129 * (h_ % 2)
            sbanks = (pkh[0], pkh[1], pbf[0], pbf[1])
            for g in range(4):
                Eg = 8 * g + 8
                cstg = cst[g % 2]
                expbase = b.c("aC")
                ctxbase = b.c("tX")
                bcol = 4 * h_ + g
                tsh = b.sem("tS")
                ach = b.sem("aC")
                txh = b.sem("tX")
                PD = 3
                exp_vs, msk_vs = {}, {}

                def _ilo(kt):
                    r = kt - 8 * g
                    return (r // 2 if r >= 0 else 0), r

                for pos in range(Eg + PD):
                    if pos < Eg:
                        kt = pos
                        ph = kt % 4
                        ilo, r = _ilo(kt)
                        co = 128 * ilo
                        b.wait("tensor", "aC", expbase + kt - 3)
                        b.on("tensor", lambda e, kb=kb, kt=kt, p=ph, co=co, h=tsh, hh=h_, g=g: e.matmul(
                            sbanks[p][:, co:512], kb[:, 128 * kt:128 * (kt + 1)],
                            q_fm[:, TQ * hh + 512 * g + co: TQ * hh + 512 * (g + 1)],
                            start=True, stop=True).then_inc(h, 1))
                        st_v = b.inc("tS")
                        b.wait("scalar", "tS", st_v)
                        b.wait("scalar", "tX", ctxbase + kt - 3)
                        b.on("scalar", lambda e, p=ph, co=co, a=att[ph], c=bcol, h=ach: e.activation(
                            a[:, co:512], sbanks[p][:, co:512], AF.Exp, bias=bias_all[:, c:c + 1], scale=SCALE).then_inc(h, 1))
                        exp_vs[kt] = b.inc("aC")
                        if r >= 0:
                            b.wait("vector", "aC", exp_vs[kt])
                            vmh = b.sem("vM")
                            b.on("vector", lambda e, a=att[ph], r=r, co=co, h=vmh: e.tensor_tensor(a[:, co:512], a[:, co:512], masks[:, 512 * r + co:512 * (r + 1)], ALU.mult).then_inc(h, 1))
                            msk_vs[kt] = b.inc("vM")
                    kd = pos - PD
                    if 0 <= kd < Eg:
                        ph = kd % 4
                        ilo, r = _ilo(kd)
                        if r >= 0:
                            b.wait("tensor", "vM", msk_vs[kd])
                        else:
                            b.wait("tensor", "aC", exp_vs[kd])
                        if kd == 0:
                            b.wait("tensor", "vD", ctx_slot_war[0])
                        slots = list(range(ilo, 4))
                        for i in slots:
                            pslc = pv[:, 512 * i:512 * i + 129]
                            lhs = att[ph][:, 128 * i:128 * (i + 1)]
                            rhs = vslice[:, kd, voff:voff + 129]
                            stop_i = (kd == 8 * g + 2 * i + 1)
                            if i == slots[-1]:
                                b.on("tensor", lambda e, p=pslc, l=lhs, r=rhs, kd=kd, st=stop_i, h=txh: e.matmul(p, l, r, start=(kd == 0), stop=st).then_inc(h, 1))
                            else:
                                b.on("tensor", lambda e, p=pslc, l=lhs, r=rhs, kd=kd, st=stop_i: e.matmul(p, l, r, start=(kd == 0), stop=st))
                        b.inc("tX")
                        ie = (kd - 8 * g - 1) // 2
                        if kd >= 8 * g + 1 and kd == 8 * g + 2 * ie + 1 and 0 <= ie < 4:
                            b.wait("vector", "tX", b.c("tX"))
                            if ie == 0:
                                b.waitv("vector", cst_war[g % 2])
                            vdh2 = b.sem("vD")
                            rl_v = b.vops([
                                lambda e, i=ie: e.tensor_scalar(rlb[:, i:i + 1], pv[:, 512 * i + 128:512 * i + 129], GUARD, None, ALU.add),
                                lambda e, i=ie: e.reciprocal(rlc[:, i:i + 1], rlb[:, i:i + 1]),
                            ])
                            b.wait("vector", "vX", rl_v)
                            b.on("vector", lambda e, i=ie, c=cstg, h=vdh2: e.tensor_scalar(c[:, 128 * i:128 * (i + 1)], pv[:, 512 * i:512 * i + 128], rlc[:, i:i + 1], None, ALU.mult).then_inc(h, 1))
                            b.inc("vD")
                ctx_slot_war[0] = b.c("vD")
                b.wait("gpsimd", "vD", ctx_slot_war[0])
                dstc = CTX[h_, 512 * g:512 * (g + 1), :].rearrange("(i p) c -> p i c", p=128)
                cst_war[g % 2] = b.dma("gpsimd", "sCW", dstc, cstg[:].rearrange("p (i c) -> p i c", c=128))
                emit_pj_chunk(4)
            ts_after_head.append(b.c("tS"))
            tx_after_head.append(b.c("tX"))
            if h_ + 2 < H:
                b.wait("sync", "tS", ts_after_head[h_])
                kh_rdy.append(b.dma("sync", "sKH", kbuf[h_ % 2][:], Ks[h_ + 2, :, :]))
            if h_ % 2 == 1 and (h_ + 3) // 2 < 8:
                p_ = (h_ + 3) // 2
                b.wait("sync", "tX", tx_after_head[h_])
                vp_rdy.append(b.dma("sync", "sVP", vbuf[p_ % 2][:].rearrange("p (k c) -> p k c", c=258), vsr[:, :, 258 * p_:258 * (p_ + 1)]))
        emit_pj_chunk(256)

        p3s.close()
        b.barrier()

        p35s = ExitStack()
        wot_b = b.sb("wot_b", [128, 16 * E], BF16, p35s)
        woo = woT.rearrange("(dc p) o -> p dc o", p=128)
        wo_rdy = b.dma("gpsimd", "sWO", wot_b[:].rearrange("p (dc o) -> p dc o", o=E), woo)
        bo_b = b.sb("bo_b", [128, E], F32, p35s)
        cstm = [b.sb("cstm0", [128, 2048], BF16, p35s), b.sb("cstm1", [128, 2048], BF16, p35s)]
        cfm = b.sb("cfm", [128, 2048], BF16, p35s)
        xqs = [b.sb("xqs0", [128, E], F32, p35s), b.sb("xqs1", [128, E], F32, p35s)]
        aos = [b.sb("aos0", [128, E], F32, p35s), b.sb("aos1", [128, E], F32, p35s)]
        bo_rdy = b.dma("sync", "sIN", bo_b[:], bo.partition_broadcast(128))
        b.waitv("tensor", wo_rdy)
        b.wait_all("sync", "sCW")
        tr35 = [b.c("vB"), b.c("vB")]
        pv35 = [b.c("vD"), b.c("vD")]
        cfm_war = [0]
        aos_war = [None, None]
        xq_war = [0, 0]
        tt_tT = [0] * 16
        for tt in range(16):
            cbuf = cstm[tt % 2]
            if tt >= 2:
                b.wait("sync", "tT", tt_tT[tt - 2])
            lv = 0
            for h_ in range(H):
                lv = b.dma("sync", "sCR", cbuf[:, 128 * h_:128 * (h_ + 1)], CTX[h_, 128 * tt:128 * (tt + 1), :])
            b.wait("sync", "vD", xq_war[tt % 2])
            xb = xqs[tt % 2]
            xv = b.dma("sync", "sAQ", xb[:], xq[128 * tt:128 * (tt + 1), :])
            b.waitv("tensor", lv)
            b.wait("tensor", "tP", b.c("tP"))
            tth = b.sem("tT")
            vbh = b.sem("vB")
            for jh in range(2):
                pt = ptrs[jh]
                b.wait("tensor", "vB", tr35[jh])
                for j in range(8):
                    jj = jh * 8 + j
                    if j == 7:
                        b.on("tensor", lambda e, pt=pt, j=j, jj=jj, c=cbuf, h=tth: e.transpose(pt[:, 128 * j:128 * (j + 1)], c[:, 128 * jj:128 * (jj + 1)], ident[:]).then_inc(h, 1))
                    else:
                        b.on("tensor", lambda e, pt=pt, j=j, jj=jj, c=cbuf: e.transpose(pt[:, 128 * j:128 * (j + 1)], c[:, 128 * jj:128 * (jj + 1)], ident[:]))
                tv = b.inc("tT")
                b.wait("vector", "tT", tv)
                dst = cfm[:].rearrange("p (j c) -> p j c", c=128)[:, 8 * jh:8 * jh + 8, :]
                b.on("vector", lambda e, pt=pt, d=dst, h=vbh: e.tensor_copy(d, pt[:].rearrange("p (j c) -> p j c", c=128)).then_inc(h, 1))
                tr35[jh] = b.inc("vB")
            tt_tT[tt] = b.c("tT")
            b.wait("tensor", "vB", b.c("vB"))
            for eh in range(2):
                b.wait("tensor", "vD", pv35[eh])
                tph = b.sem("tP")
                for dh in range(16):
                    for s2 in range(2):
                        lhs = cfm[:, 128 * dh:128 * (dh + 1)]
                        rhs = wot_b[:, E * dh + 1024 * eh + 512 * s2: E * dh + 1024 * eh + 512 * (s2 + 1)]
                        pslc = pv[:, 1024 * eh + 512 * s2: 1024 * eh + 512 * (s2 + 1)]
                        if dh == 15 and s2 == 1:
                            b.on("tensor", lambda e, p=pslc, l=lhs, r=rhs, h=tph: e.matmul(p, l, r, start=False, stop=True).then_inc(h, 1))
                        else:
                            b.on("tensor", lambda e, p=pslc, l=lhs, r=rhs, dh=dh, s2=s2: e.matmul(p, l, r, start=(dh == 0), stop=(dh == 15 and s2 == 1)))
                tv = b.inc("tP")
                b.wait("vector", "tP", tv)
                if tt == 0 and eh == 0:
                    b.waitv("vector", bo_rdy)
                b.waitv("vector", xv)
                b.waitv("vector", aos_war[tt % 2])
                ab = aos[tt % 2]
                vdh = b.sem("vD")
                b.on("vector", lambda e, a=ab, eh=eh, h=vdh: e.tensor_tensor(a[:, 1024 * eh:1024 * (eh + 1)], pv[:, 1024 * eh:1024 * (eh + 1)], bo_b[:, 1024 * eh:1024 * (eh + 1)], ALU.add).then_inc(h, 1))
                b.inc("vD")
                b.on("vector", lambda e, a=ab, x=xb, eh=eh, h=vdh: e.tensor_tensor(a[:, 1024 * eh:1024 * (eh + 1)], a[:, 1024 * eh:1024 * (eh + 1)], x[:, 1024 * eh:1024 * (eh + 1)], ALU.add).then_inc(h, 1))
                pv35[eh] = b.inc("vD")
            xq_war[tt % 2] = b.c("vD")
            b.wait("sync", "vD", b.c("vD"))
            aos_war[tt % 2] = b.dma("sync", "sAW", AOUT[128 * tt:128 * (tt + 1), :], aos[tt % 2][:])

        p35s.close()
        p3os.close()
        p23s.close()
        b.barrier()

        p4s = ExitStack()
        x1fm = b.sb("x1fm", [128, 16 * TQ], BF16, p4s)
        xt4a = b.sb("xt4a", [128, E], F32, p4s)
        xt4b = b.sb("xt4b", [128, E], F32, p4s)
        x1t = b.sb("x1t", [128, E], BF16, p4s)
        twfc = [b.sb("twfc0", [128, 16 * 512], BF16, p4s), b.sb("twfc1", [128, 16 * 512], BF16, p4s)]
        fcs = [b.sb("fcs0", [128, 512], F32, p4s), b.sb("fcs1", [128, 512], F32, p4s)]
        gls = [b.sb("gls0", [128, 512], F32, p4s), b.sb("gls1", [128, 512], F32, p4s)]
        xts4 = [xt4a, xt4b]

        b.wait("vector", "sGB", gb_rdy)
        tr4 = [b.c("vB"), b.c("vB")]
        xw4 = [0, 0]
        x1t_war = [0]
        for tt in range(16):
            xb = xts4[tt % 2]
            b.wait("sync", "vX", xw4[tt % 2])
            lv = b.dma("sync", "sQX", xb[:], xq[128 * tt:128 * (tt + 1), :])
            b.waitv("vector", lv)
            b.wait("vector", "tT", x1t_war[0])
            cc = 4 * tt
            x1_v = b.vops([
                lambda e, c=cc: e.tensor_tensor(tmp1[:, 0:1], stats[:, c + 2:c + 3], gfc_b, ALU.mult),
                lambda e, c=cc: e.tensor_tensor(tmp1[:, 1:2], stats[:, c + 1:c + 2], tmp1[:, 0:1], ALU.mult),
                lambda e: e.tensor_scalar(tmp1[:, 2:3], tmp1[:, 0:1], CLIPB, None, ALU.mult),
                lambda e: e.tensor_scalar(tmp1[:, 3:4], tmp1[:, 2:3], -1.0, None, ALU.mult),
                lambda e, xb=xb, c=cc: e.tensor_scalar(x1t[:], xb[:], stats[:, c:c + 1], tmp1[:, 1:2], ALU.subtract, ALU.mult),
                lambda e: e.tensor_scalar(x1t[:], x1t[:], tmp1[:, 2:3], None, ALU.min),
                lambda e: e.tensor_scalar(x1t[:], x1t[:], tmp1[:, 3:4], None, ALU.max, ALU.add, accum_out=tmp1[:, 4:5]),
            ])
            xw4[tt % 2] = x1_v
            b.vops([
                lambda e, c=cc: e.tensor_scalar(stats[:, c + 3:c + 4], tmp1[:, 4:5], -1.0, None, ALU.mult),
            ])
            b.wait("tensor", "vX", x1_v)
            tth = b.sem("tT")
            vbh = b.sem("vB")
            for jh in range(2):
                pt = ptrs[jh]
                b.wait("tensor", "vB", tr4[jh])
                for j in range(8):
                    jj = jh * 8 + j
                    if j == 7:
                        b.on("tensor", lambda e, pt=pt, j=j, jj=jj, h=tth: e.transpose(pt[:, 128 * j:128 * (j + 1)], x1t[:, 128 * jj:128 * (jj + 1)], ident[:]).then_inc(h, 1))
                    else:
                        b.on("tensor", lambda e, pt=pt, j=j, jj=jj: e.transpose(pt[:, 128 * j:128 * (j + 1)], x1t[:, 128 * jj:128 * (jj + 1)], ident[:]))
                tv = b.inc("tT")
                b.wait("vector", "tT", tv)
                dst = x1fm[:].rearrange("p (j c) -> p j c", c=TQ)[:, 8 * jh:8 * jh + 8, 128 * tt:128 * (tt + 1)]
                b.on("vector", lambda e, pt=pt, d=dst, h=vbh: e.tensor_copy(d, pt[:].rearrange("p (j c) -> p j c", c=128)).then_inc(h, 1))
                tr4[jh] = b.inc("vB")
            x1t_war[0] = b.c("tT")
        b.wait("tensor", "vB", b.c("vB"))

        pvq_war = [b.c("aB")] * 4
        fcg_war = [0, 0]
        fc_toks = [0, 0]
        gls_war = [None, None]
        for oc in range(16):
            tw = twfc[oc % 2]
            for d in range(16):
                it = oc * 16 + d
                fb = fcs[it % 2]
                if it >= 2:
                    b.wait("gpsimd", "vX", fc_toks[it % 2])
                lv = b.dma("gpsimd", "sFC", fb[:], wfcT[128 * d:128 * (d + 1), 512 * oc:512 * (oc + 1)])
                b.waitv("vector", lv)
                b.wait("vector", "tG", fcg_war[oc % 2])
                dstt = tw[:, 512 * d:512 * (d + 1)]
                fc_toks[it % 2] = b.vops([
                    lambda e, a=fb, t=dstt: e.tensor_single_scalar(t, a[:], thfc_b, ALU.is_gt),
                    lambda e, a=fb, t=dstt: e.scalar_tensor_tensor(t, a[:], nthfc_b, t, ALU.is_ge, ALU.add),
                ])
            b.wait("tensor", "vX", b.c("vX"))
            for tt in range(16):
                it = oc * 16 + tt
                qd = it % 4
                b.wait("tensor", "aB", pvq_war[qd])
                tgh = b.sem("tG")
                for d in range(16):
                    lhs = x1fm[:, TQ * d + 128 * tt: TQ * d + 128 * (tt + 1)]
                    rhs = tw[:, 512 * d:512 * (d + 1)]
                    pslc = pv[:, 512 * qd:512 * (qd + 1)]
                    if d == 15:
                        b.on("tensor", lambda e, p=pslc, l=lhs, r=rhs, h=tgh: e.matmul(p, l, r, start=False, stop=True).then_inc(h, 1))
                    else:
                        b.on("tensor", lambda e, p=pslc, l=lhs, r=rhs, d=d: e.matmul(p, l, r, start=(d == 0), stop=False))
                tv = b.inc("tG")
                b.wait("scalar", "tG", tv)
                gb_ = gls[it % 2]
                b.waitv("scalar", gls_war[it % 2])
                abh = b.sem("aB")
                b.on("scalar", lambda e, g=gb_, qd=qd, c=4 * tt + 3, h=abh: e.activation(g[:], pv[:, 512 * qd:512 * (qd + 1)], AF.Gelu, bias=stats[:, c:c + 1]).then_inc(h, 1))
                pvq_war[qd] = b.inc("aB")
                b.wait("sync", "aB", pvq_war[qd])
                gls_war[it % 2] = b.dma("sync", "sHW", Hs[128 * tt:128 * (tt + 1), 512 * oc:512 * (oc + 1)], gb_[:])
            fcg_war[oc % 2] = b.c("tG")

        p4s.close()
        b.barrier()

        p5s = ExitStack()
        hst = b.sb("hst", [128, FF], F32, p5s)
        X2T = [b.sb(f"x2t{i}", [128, FF], BF16, p5s) for i in range(4)]
        x2fm = b.sb("x2fm", [128, 64 * 512], BF16, p5s)
        NR = 16
        ring = [b.sb(f"ring{i}", [128, 512], BF16, p5s)[:] for i in range(NR)]
        aosb = [b.sb(f"aq{i}", [128, 512], F32, p5s) for i in range(4)]
        ostb = [b.sb(f"oq{i}", [128, 512], F32, p5s) for i in range(4)]
        pvq = [pv[:, 512 * i:512 * (i + 1)] for i in range(4)]

        b.wait_all("sync", "sTW")
        b.wait_all("gpsimd", "sTW")
        b.wait_all("scalar", "sHW")

        ring_war = [0] * NR
        ring_tok = {}
        evac_aB = [0] * 4
        hst_war = [0]
        x2t_rdy = [0] * 4
        x2t_war = [0] * 4
        st2_v = [0] * 16
        tr_aP = [0, 0]
        ostb_war = [None] * 4
        aosb_war = [0] * 4
        kfin = [0]

        def pf_ring(fi):
            if fi >= 256:
                return
            och4 = (fi // 16) % 4
            og = fi % 16
            eng, nm = ("sync", "sTR") if og % 2 == 0 else ("gpsimd", "sTG")
            pairs = []
            for oo in range(4):
                o = 4 * og + oo
                b.wait(eng, "tF", ring_war[(4 * fi + oo) % NR])
                pairs.append((ring[(4 * fi + oo) % NR],
                              TWPJ[128 * o:128 * (o + 1), 512 * och4:512 * (och4 + 1)]))
            ring_tok[fi] = b.dma_group(eng, nm, pairs, nring=4)

        def emit_subB(tt):
            tt4 = tt % 4
            xt = X2T[tt4]
            b.wait("scalar", "vX", hst_war[0])
            toks = [b.dma_group("scalar", "sHR", [
                (hst[:, 1024 * hf:1024 * (hf + 1)],
                 Hs[128 * tt:128 * (tt + 1), 1024 * hf:1024 * (hf + 1)])
                for hf in range(4 * gg, 4 * gg + 4)], nring=4) for gg in range(2)]
            for t in toks:
                b.waitv("vector", t)
            b.wait("vector", "vX", b.c("vX"))
            vxh = b.sem("vX")
            for g in range(15):
                b.on("vector", lambda e, g=g: e.bn_stats(bnst[:, 6 * g:6 * g + 6], hst[:, 512 * g:512 * (g + 1)]))
            b.on("vector", lambda e, h=vxh: e.bn_stats(bnst[:, 90:96], hst[:, 7680:8192]).then_inc(h, 1))
            b.inc("vX")
            den_v = b.vops([
                lambda e: e.bn_aggr(bnag[:, 0:2], bnst[:, 0:48]),
                lambda e: e.bn_aggr(bnag[:, 2:4], bnst[:, 48:96]),
                lambda e: e.tensor_tensor(tmp1[:, 0:1], bnag[:, 0:1], bnag[:, 2:3], ALU.add),
                lambda e: e.tensor_scalar(tmp1[:, 0:1], tmp1[:, 0:1], 0.5, None, ALU.mult),
                lambda e: e.tensor_tensor(tmp1[:, 1:2], bnag[:, 0:1], bnag[:, 2:3], ALU.subtract),
                lambda e: e.tensor_tensor(tmp1[:, 1:2], tmp1[:, 1:2], tmp1[:, 1:2], ALU.mult),
                lambda e: e.tensor_tensor(tmp1[:, 2:3], bnag[:, 1:2], bnag[:, 3:4], ALU.add),
                lambda e: e.tensor_scalar(tmp1[:, 2:3], tmp1[:, 2:3], 0.5, None, ALU.mult),
                lambda e: e.tensor_scalar(tmp1[:, 1:2], tmp1[:, 1:2], 0.25, None, ALU.mult),
                lambda e: e.tensor_tensor(tmp1[:, 2:3], tmp1[:, 2:3], tmp1[:, 1:2], ALU.add),
                lambda e: e.tensor_reduce(tmp1[:, 4:5], hst[:, 0:4096], AX, ALU.max),
                lambda e: e.tensor_reduce(tmp1[:, 5:6], hst[:, 4096:8192], AX, ALU.max),
                lambda e: e.tensor_reduce(tmp1[:, 6:7], hst[:, 0:4096], AX, ALU.min),
                lambda e: e.tensor_reduce(tmp1[:, 7:8], hst[:, 4096:8192], AX, ALU.min),
                lambda e: e.tensor_tensor(tmp1[:, 4:5], tmp1[:, 4:5], tmp1[:, 5:6], ALU.max),
                lambda e: e.tensor_tensor(tmp1[:, 6:7], tmp1[:, 6:7], tmp1[:, 7:8], ALU.min),
                lambda e: e.tensor_scalar(tmp1[:, 3:4], tmp1[:, 2:3], 1e-6, None, ALU.add),
            ])
            b.wait("scalar", "vX", den_v)
            r2_v = b.vops([
                lambda e: e.activation(tmp1[:, 8:9], tmp1[:, 3:4], AF.Ln),
                lambda e: e.activation(tmp1[:, 9:10], tmp1[:, 8:9], AF.Exp, scale=-0.5),
            ], eng="scalar")
            b.wait("vector", "aX", r2_v)
            b.wait("vector", "tT", x2t_war[tt4])
            x2_v = b.vops([
                lambda e: e.tensor_tensor(tmp1[:, 4:5], tmp1[:, 4:5], tmp1[:, 0:1], ALU.subtract),
                lambda e: e.tensor_tensor(tmp1[:, 6:7], tmp1[:, 0:1], tmp1[:, 6:7], ALU.subtract),
                lambda e: e.tensor_tensor(tmp1[:, 4:5], tmp1[:, 4:5], tmp1[:, 6:7], ALU.max),
                lambda e: e.tensor_tensor(tmp1[:, 11:12], tmp1[:, 9:10], gpj_b, ALU.mult),
                lambda e: e.scalar_tensor_tensor(tmp1[:, 10:11], tmp1[:, 4:5], 1.0 / 128.0, tmp1[:, 9:10], ALU.mult, ALU.mult),
                lambda e: e.tensor_tensor(tmp1[:, 10:11], tmp1[:, 10:11], gpj_b, ALU.mult),
                lambda e: e.tensor_scalar(tmp1[:, 12:13], tmp1[:, 10:11], CLIPB, None, ALU.mult),
                lambda e: e.tensor_scalar(tmp1[:, 13:14], tmp1[:, 10:11], -CLIPB, None, ALU.mult),
                lambda e, xt=xt: e.tensor_scalar(xt[:], hst[:], tmp1[:, 0:1], tmp1[:, 11:12], ALU.subtract, ALU.mult),
                lambda e, xt=xt: e.tensor_scalar(xt[:], xt[:], tmp1[:, 12:13], None, ALU.min),
                lambda e, xt=xt: e.tensor_scalar(xt[:], xt[:], tmp1[:, 13:14], None, ALU.max, ALU.add, accum_out=tmp1[:, 14:15]),
            ])
            hst_war[0] = x2_v
            st2_v[tt] = b.vops([
                lambda e, c=4 * tt: e.tensor_scalar(stats2[:, c + 3:c + 4], tmp1[:, 14:15], -1.0, None, ALU.mult),
            ])
            x2t_rdy[tt4] = x2_v

        def emit_transposes(q):
            for tt4 in range(4):
                b.wait("tensor", "vX", x2t_rdy[tt4])
                xt = X2T[tt4]
                for j8 in range(8):
                    pt = ptrs[j8 % 2]
                    b.wait("tensor", "aP", tr_aP[j8 % 2])
                    tth = b.sem("tT")
                    for j in range(8):
                        jj = j8 * 8 + j
                        if j == 7:
                            b.on("tensor", lambda e, pt=pt, j=j, jj=jj, xt=xt, h=tth: e.transpose(pt[:, 128 * j:128 * (j + 1)], xt[:, 128 * jj:128 * (jj + 1)], ident[:]).then_inc(h, 1))
                        else:
                            b.on("tensor", lambda e, pt=pt, j=j, jj=jj, xt=xt: e.transpose(pt[:, 128 * j:128 * (j + 1)], xt[:, 128 * jj:128 * (jj + 1)], ident[:]))
                    tv = b.inc("tT")
                    b.wait("scalar", "tT", tv)
                    aph = b.sem("aP")
                    dst = x2fm[:].rearrange("p (o c) -> p o c", c=512)[:, 8 * j8:8 * j8 + 8, 128 * tt4:128 * (tt4 + 1)]
                    b.on("scalar", lambda e, pt=pt, d=dst, h=aph: e.activation(d, pt[:].rearrange("p (j c) -> p j c", c=128), AF.Copy).then_inc(h, 1))
                    tr_aP[j8 % 2] = b.inc("aP")
                x2t_war[tt4] = b.c("tT")

        def emit_og_pass(q, och4):
            p = 4 * q + och4
            base_fi = 16 * p
            if p == 0:
                for k in range(4):
                    pf_ring(k)
            b.wait("tensor", "aP", b.c("aP"))
            for og in range(16):
                fi = base_fi + og
                b.waitv("tensor", ring_tok[fi])
                tfh = b.sem("tF")
                nmm = 0
                for tt4 in range(4):
                    if og == 0:
                        b.wait("tensor", "aB", evac_aB[tt4])
                    for oo in range(4):
                        o = 4 * og + oo
                        lhs = x2fm[:, 512 * o + 128 * tt4: 512 * o + 128 * (tt4 + 1)]
                        rhs = ring[(4 * fi + oo) % NR]
                        st = (og == 0 and oo == 0)
                        sp = (og == 15 and oo == 3)
                        nmm += 1
                        if nmm == 16:
                            b.on("tensor", lambda e, pq=pvq[tt4], l=lhs, r=rhs, st=st, sp=sp, h=tfh: e.matmul(pq, l, r, start=st, stop=sp).then_inc(h, 1))
                        else:
                            b.on("tensor", lambda e, pq=pvq[tt4], l=lhs, r=rhs, st=st, sp=sp: e.matmul(pq, l, r, start=st, stop=sp))
                tv = b.inc("tF")
                for oo in range(4):
                    ring_war[(4 * fi + oo) % NR] = tv
                pf_ring(fi + 4)

        def emit_finals(q, och4):
            b.wait("scalar", "tF", b.c("tF"))
            for tt4 in range(4):
                tt = 4 * q + tt4
                k = kfin[0]
                kfin[0] += 1
                ob = ostb[k % 4]
                a = k % 4
                b.wait("scalar", "vD", aosb_war[a])
                atok = b.dma("scalar", "sAR", aosb[a][:], AOUT[128 * tt:128 * (tt + 1), 512 * och4:512 * (och4 + 1)])
                b.waitv("scalar", ostb_war[k % 4])
                b.wait("scalar", "vX", st2_v[tt])
                abh = b.sem("aB")
                b.on("scalar", lambda e, ob=ob, pq=pvq[tt4], c=4 * tt + 3, h=abh: e.activation(ob[:], pq, AF.Identity, bias=stats2[:, c:c + 1]).then_inc(h, 1))
                evac_aB[tt4] = b.inc("aB")
                b.wait("vector", "aB", evac_aB[tt4])
                b.waitv("vector", atok)
                vdh = b.sem("vD")
                b.on("vector", lambda e, ob=ob, a=a, h=vdh: e.tensor_tensor(ob[:], ob[:], aosb[a][:], ALU.add).then_inc(h, 1))
                ov = b.inc("vD")
                aosb_war[a] = ov
                b.wait("sync", "vD", ov)
                ostb_war[k % 4] = b.dma("sync", "sOU", out[128 * tt:128 * (tt + 1), 512 * och4:512 * (och4 + 1)], ob[:])

        for tt in range(4):
            emit_subB(tt)
        emit_transposes(0)
        for q in range(4):
            for och4 in range(4):
                if q + 1 < 4:
                    emit_subB(4 * (q + 1) + och4)
                emit_og_pass(q, och4)
                if och4 == 3 and q + 1 < 4:
                    emit_transposes(q + 1)
                emit_finals(q, och4)

        p5s.close()
        b.wait_all("sync", "sOU")

        with nc.Block() as block:
            @block.sync
            def _(e):
                for f in b.q["sync"]:
                    f(e)

            @block.gpsimd
            def _(e):
                for f in b.q["gpsimd"]:
                    f(e)

            @block.vector
            def _(e):
                for f in b.q["vector"]:
                    f(e)

            @block.scalar
            def _(e):
                for f in b.q["scalar"]:
                    f(e)

            @block.tensor
            def _(e):
                for f in b.q["tensor"]:
                    f(e)

    return nc


def _make_masks(half):
    pt = np.zeros((8, 128, 512), dtype=np.float32)
    tri = np.triu(np.ones((128, 128), dtype=np.float32))
    for r in range(8):
        for i in range(4):
            d = 2 * i + half
            if r < d:
                pt[r][:, 128 * i:128 * (i + 1)] = 1.0
            elif r == d:
                pt[r][:, 128 * i:128 * (i + 1)] = tri
    return pt.astype(bfdt)


_BUILT = {}


def kernel(L0_attn_x, L0_attn_Wq, L0_attn_Wk, L0_attn_Wv, L0_attn_Wo,
           L0_attn_bo, L0_ffn_w_fc, L0_ffn_w_proj, _debug=False):
    x = np.asarray(L0_attn_x, dtype=np.float32)
    wqT = np.ascontiguousarray(np.asarray(L0_attn_Wq, np.float32).T)
    wkT = np.ascontiguousarray(np.asarray(L0_attn_Wk, np.float32).T)
    wvT = np.ascontiguousarray(np.asarray(L0_attn_Wv, np.float32).T)
    woT = np.ascontiguousarray(np.asarray(L0_attn_Wo, np.float32).T)
    wfcT = np.ascontiguousarray(np.asarray(L0_ffn_w_fc, np.float32).T)
    wpjT = np.ascontiguousarray(np.asarray(L0_ffn_w_proj, np.float32).T)
    bo_ = np.ascontiguousarray(np.asarray(L0_attn_bo, np.float32).reshape(1, E))
    ident = np.eye(128, dtype=np.float32).astype(bfdt)

    key = bool(_debug)
    if key not in _BUILT:
        _BUILT[key] = build_nc(debug=_debug)
    nc = _BUILT[key]

    in_maps = []
    for c in range(8):
        bidx, half = c // 2, c % 2
        rows = np.concatenate(
            [x[bidx, 128 * (2 * j + half):128 * (2 * j + half) + 128] for j in range(16)], axis=0)
        in_maps.append({
            "xf": np.ascontiguousarray(x[bidx]), "xq": np.ascontiguousarray(rows),
            "wqT": wqT, "wkT": wkT, "wvT": wvT, "woT": woT,
            "wfcT": wfcT, "wpjT": wpjT, "bo": bo_,
            "ptm": _make_masks(half), "idn": ident,
        })

    res = run_bass_kernel_spmd(nc, in_maps, core_ids=list(range(8)))

    outp = np.empty((B, S, E), dtype=np.float32)
    for c in range(8):
        bidx, half = c // 2, c % 2
        o = res.results[c]["out"]
        for j in range(16):
            outp[bidx, 128 * (2 * j + half):128 * (2 * j + half) + 128] = o[128 * j:128 * (j + 1)]
    if _debug:
        return outp, res
    return outp


# version 20
# speedup vs baseline: 1.1441x; 1.1441x over previous
import numpy as np
import ml_dtypes
from contextlib import ExitStack

import concourse.bass as bass
import concourse.mybir as mybir
from concourse.bass_utils import run_bass_kernel_spmd

F32 = mybir.dt.float32
BF16 = mybir.dt.bfloat16
AX = mybir.AxisListType.X
ALU = mybir.AluOpType
AF = mybir.ActivationFunctionType
bfdt = ml_dtypes.bfloat16

B, S, E, H, DH, FF = 4, 4096, 2048, 16, 128, 8192
TQ = 2048
SCALE = float(1.0 / np.sqrt(np.float64(128.0)))
CLIPB = float(np.float32(128.0 - 1e-6))
GUARD = 1e-30

ENGS = ("sync", "gpsimd", "vector", "scalar", "tensor")


class Bld:
    def __init__(self, nc):
        self.nc = nc
        self.q = {k: [] for k in ENGS}
        self.stack = ExitStack()
        self.sems = {}
        self.cnt = {}

    def on(self, eng, fn):
        self.q[eng].append(fn)

    def sem(self, name):
        if name not in self.sems:
            self.sems[name] = self.stack.enter_context(self.nc.semaphore(name))
            self.cnt[name] = 0
        return self.sems[name]

    def inc(self, name, k=1):
        self.sem(name)
        self.cnt[name] += k
        return self.cnt[name]

    def c(self, name):
        return self.cnt.get(name, 0)

    def wait(self, eng, name, v):
        if v > 0:
            h = self.sem(name)
            self.on(eng, lambda e, h=h, v=v: e.wait_ge(h, v))

    def barrier(self):
        snap = [(self.sems[n], self.cnt[n]) for n in self.sems if self.cnt[n] > 0]
        for eng in ENGS:
            for h, v in snap:
                self.on(eng, lambda e, h=h, v=v: e.wait_ge(h, v))

    def sb(self, name, shape, dtype, stk=None):
        return (stk or self.stack).enter_context(self.nc.sbuf_tensor(name, shape, dtype))

    def dma(self, eng, name, dst, src):
        if not hasattr(self, "ring_idx"):
            self.ring_idx = {}
        nring = {"sTR": 4, "sOU": 4, "sAR": 4, "sHR": 4, "sPJ": 4}.get(name, 2)
        i = self.ring_idx[name] = (self.ring_idx.get(name, -1) + 1) % nring
        sub = f"{name}~{i}"
        h = self.sem(sub)
        self.on(eng, lambda e, d=dst, s=src, h=h: e.dma_start(out=d, in_=s).then_inc(h, 16))
        return (sub, self.inc(sub, 16))

    def dma_group(self, eng, name, pairs, nring=8):
        if not hasattr(self, "ring_idx"):
            self.ring_idx = {}
        i = self.ring_idx[name] = (self.ring_idx.get(name, -1) + 1) % nring
        sub = f"{name}~{i}"
        h = self.sem(sub)
        for dst, src in pairs:
            self.on(eng, lambda e, d=dst, s=src, h=h: e.dma_start(out=d, in_=s).then_inc(h, 16))
            self.inc(sub, 16)
        return (sub, self.c(sub))

    def waitv(self, eng, tok):
        if tok:
            self.wait(eng, tok[0], tok[1])

    def wait_all(self, eng, name):
        for i in range(8):
            sub = f"{name}~{i}"
            if sub in self.sems and self.cnt[sub] > 0:
                self.wait(eng, sub, self.cnt[sub])

    def vops(self, ops, eng="vector", sem=None):
        sem = sem or ("vX" if eng == "vector" else "aX")
        h = self.sem(sem)
        for fn in ops:
            self.wait(eng, sem, self.c(sem))
            self.on(eng, lambda e, fn=fn, h=h: fn(e).then_inc(h, 1))
            self.inc(sem)
        return self.c(sem)


def build_nc(debug=False):
    nc = bass.Bass("TRN2", target_bir_lowering=False)
    b = Bld(nc)

    xf = nc.dram_tensor("xf", [S, E], F32, kind="ExternalInput").ap()
    xq = nc.dram_tensor("xq", [TQ, E], F32, kind="ExternalInput").ap()
    wqT = nc.dram_tensor("wqT", [E, E], F32, kind="ExternalInput").ap()
    wkT = nc.dram_tensor("wkT", [E, E], F32, kind="ExternalInput").ap()
    wvT = nc.dram_tensor("wvT", [E, E], F32, kind="ExternalInput").ap()
    woT = nc.dram_tensor("woT", [E, E], F32, kind="ExternalInput").ap()
    wfcT = nc.dram_tensor("wfcT", [E, FF], F32, kind="ExternalInput").ap()
    wpjT = nc.dram_tensor("wpjT", [FF, E], F32, kind="ExternalInput").ap()
    bo = nc.dram_tensor("bo", [1, E], F32, kind="ExternalInput").ap()
    ptm = nc.dram_tensor("ptm", [8, 128, 512], BF16, kind="ExternalInput").ap()
    idn = nc.dram_tensor("idn", [128, 128], BF16, kind="ExternalInput").ap()
    out = nc.dram_tensor("out", [TQ, E], F32, kind="ExternalOutput").ap()

    ks = "ExternalOutput" if debug else "Internal"
    XNF = nc.dram_tensor("XNF", [16, 128, S], BF16, kind=ks).ap()
    Ks = nc.dram_tensor("Ks", [H, 128, S], BF16, kind=ks).ap()
    Vs = nc.dram_tensor("Vs", [S, H * 129], BF16, kind=ks).ap()
    CTX = nc.dram_tensor("CTX", [H, TQ, DH], BF16, kind=ks).ap()
    AOUT = nc.dram_tensor("AOUT", [TQ, E], F32, kind=ks).ap()
    Hs = nc.dram_tensor("Hs", [TQ, FF], F32, kind=ks).ap()
    TWPJ = nc.dram_tensor("TWPJ", [FF, E], BF16, kind=ks).ap()
    QDBG = nc.dram_tensor("QDBG", [128, H * TQ], BF16, kind=ks).ap() if debug else None
    GDBG = nc.dram_tensor("GDBG", [128, 16], F32, kind=ks).ap() if debug else None

    with b.stack:
        pk = b.stack.enter_context(nc.psum_tensor("pk", [128, 1024], F32))
        ptr0 = b.stack.enter_context(nc.psum_tensor("ptr0", [128, 1024], BF16))
        ptr1 = b.stack.enter_context(nc.psum_tensor("ptr1", [128, 1024], BF16))
        pv = b.stack.enter_context(nc.psum_tensor("pv", [128, 2048], F32))
        ptrs = (ptr0, ptr1)
        pkh = (pk[:, 0:512], pk[:, 512:1024])
        pbf = (ptr0.bitcast(F32).ap(), ptr1.bitcast(F32).ap())

        ident = b.sb("ident", [128, 128], BF16)
        ones_bf = b.sb("ones_bf", [128, 1], BF16)
        onesr_bf = b.sb("onesr_bf", [1, 128], BF16)
        ones_f32 = b.sb("ones_f32", [128, 1], F32)
        stats = b.sb("stats", [128, 64], F32)
        stats2 = b.sb("stats2", [128, 64], F32)
        sc0 = b.sb("sc0", [1, 64], F32)
        kmax2 = b.sb("kmax2", [1, 16], F32)
        qmax2 = b.sb("qmax2", [1, 64], F32)
        t64 = b.sb("t64", [1, 64], F32)
        tall = b.sb("tall", [1, 64], BF16)
        bias_all = b.sb("bias_all", [128, 64], F32)
        gacc = b.sb("gacc", [128, 2], F32)
        gcols = b.sb("gcols", [128, 512], F32)
        gall = b.sb("gall", [128, 6], F32)
        gfc_b = gall[:, 0:1]
        gpj_b = gall[:, 1:2]
        thfc_b = gall[:, 2:3]
        thpj_b = gall[:, 3:4]
        nthfc_b = gall[:, 4:5]
        nthpj_b = gall[:, 5:6]
        onesr_f32 = b.sb("onesr_f32", [1, 128], F32)
        tmp1 = b.sb("tmp1", [128, 32], F32)
        bnst = b.sb("bnst", [128, 96], F32)
        bnag = b.sb("bnag", [128, 8], F32)

        idn_tok = b.dma("sync", "sIN", ident[:], idn)
        b.on("gpsimd", lambda e: e.memset(ones_bf[:], 1.0))
        b.on("gpsimd", lambda e: e.memset(onesr_bf[:], 1.0))
        b.on("gpsimd", lambda e: e.memset(ones_f32[:], 1.0))
        b.on("gpsimd", lambda e: e.memset(onesr_f32[:], 1.0))
        b.on("gpsimd", lambda e: e.memset(gacc[:], 0.0))
        b.on("gpsimd", lambda e: e.memset(kmax2[:], 0.0).then_inc(b.sem("gA"), 1))
        b.inc("gA")

        gchunks = []
        for (wT, col, sname) in ((wpjT, 1, "sGP"), (wfcT, 0, "sGF")):
            wv_ = wT.rearrange("(n p) c -> n p c", p=128)
            for it in range(wv_.shape[0]):
                for ch in range(wv_.shape[2] // 2048):
                    gchunks.append((wv_[it, :, 2048 * ch:2048 * (ch + 1)], col, sname))
        gk = [0]
        gtok3 = [None, None, None]
        g_act = [0, 0, 0]

        def g_issue(kk):
            if kk >= len(gchunks):
                return
            src, col, sname = gchunks[kk]
            buf = gbb[kk % 3]
            if kk >= 3:
                b.wait("gpsimd", "aG", g_act[kk % 3])
            gtok3[kk % 3] = b.dma("gpsimd", sname, buf[:], src)

        def emit_gamma_chunks(n):
            for _ in range(n):
                kk = gk[0]
                if kk >= len(gchunks):
                    return
                if kk == 0:
                    for j in range(2):
                        g_issue(j)
                buf = gbb[kk % 3]
                b.waitv("gpsimd", gtok3[kk % 3])
                gh = b.sem("gG")
                b.wait("gpsimd", "gG", b.c("gG"))
                b.on("gpsimd", lambda e, h=gh: e.memset(stats2[0:1, 2:3], 0.0).then_inc(h, 1))
                sig = b.inc("gG")
                b.wait("scalar", "gG", sig)
                agh = b.sem("aG")
                for j in range(4):
                    cc4 = 4 * kk + j
                    if j == 3:
                        b.on("scalar", lambda e, bu=buf, j=j, c=cc4, h=agh: e.activation(bu[:, 512 * j:512 * (j + 1)], bu[:, 512 * j:512 * (j + 1)], AF.Abs, accum_out=gcols[:, c:c + 1]).then_inc(h, 1))
                    else:
                        b.on("scalar", lambda e, bu=buf, j=j, c=cc4: e.activation(bu[:, 512 * j:512 * (j + 1)], bu[:, 512 * j:512 * (j + 1)], AF.Abs, accum_out=gcols[:, c:c + 1]))
                g_act[kk % 3] = b.inc("aG")
                g_issue(kk + 2)
                gk[0] += 1

        tr_war = [0, 0]
        xnt_war = [0]

        def ln_tile(xsrc, xbuf, xnt, xscr, lsem, qti, xnf_dst_fn,
                    xbuf_war, evac_waits):
            raise NotImplementedError

        bn_war = [0, 0]

        def ln_group(srcs, qtis, xts, xnt, lsem, xw, xnf_dst_fns, first_evac_waits, per_tile_hook=None):
            nT = len(srcs)

            def stageA(i):
                pp = i % 2
                B = 16 * pp
                b.wait("sync", "vX", xw[pp])
                lv = b.dma("sync", lsem, xts[pp][:], srcs[i])
                b.waitv("vector", lv)
                b.wait("vector", "vX", bn_war[pp])
                vxh = b.sem("vX")
                for g in range(7):
                    b.on("vector", lambda e, bu=xts[pp], g=g, pp=pp: e.bn_stats(bnst[:, 48 * pp + 6 * g:48 * pp + 6 * g + 6], bu[:, 256 * g:256 * (g + 1)]))
                b.on("vector", lambda e, bu=xts[pp], pp=pp, h=vxh: e.bn_stats(bnst[:, 48 * pp + 42:48 * pp + 48], bu[:, 1792:2048]).then_inc(h, 1))
                b.inc("vX")
                is_q = qtis[i] is not None
                ops = [lambda e, pp=pp: e.bn_aggr(bnag[:, 4 * pp:4 * pp + 2], bnst[:, 48 * pp:48 * pp + 48]),
                       lambda e, pp=pp, B=B: e.tensor_scalar(tmp1[:, B:B + 1], bnag[:, 4 * pp + 1:4 * pp + 2], 1e-5, None, ALU.add)]
                if is_q:
                    ops.append(lambda e, pp=pp, B=B: e.tensor_scalar(tmp1[:, B + 1:B + 2], bnag[:, 4 * pp + 1:4 * pp + 2], 1.0 + 1e-6, None, ALU.mult))
                    ops.append(lambda e, B=B: e.tensor_scalar(tmp1[:, B + 1:B + 2], tmp1[:, B + 1:B + 2], 1e-5 * 1e-6, None, ALU.add))
                ops.append(lambda e, pp=pp, B=B: e.tensor_copy(tmp1[:, B + 2:B + 3], bnag[:, 4 * pp:4 * pp + 1]))
                den_v = b.vops(ops)
                bn_war[pp] = den_v
                return den_v

            def stageB(i, den_v):
                B = 16 * (i % 2)
                ncol = 2 if qtis[i] is not None else 1
                b.wait("scalar", "vX", den_v)
                return b.vops([
                    lambda e, n=ncol, B=B: e.activation(tmp1[:, B + 4:B + 4 + n], tmp1[:, B:B + n], AF.Ln),
                    lambda e, n=ncol, B=B: e.activation(tmp1[:, B + 6:B + 6 + n], tmp1[:, B + 4:B + 4 + n], AF.Exp, scale=-0.5),
                ], eng="scalar")

            def stageC(i, rstd_v):
                pp = i % 2
                B = 16 * pp
                xbuf = xts[pp]
                b.wait("vector", "aX", rstd_v)
                b.wait("vector", "tT", xnt_war[0])
                if i == 0:
                    for (sname, v) in first_evac_waits:
                        b.wait("vector", sname, v)
                xn_v = b.vops([lambda e, bu=xbuf, B=B: e.tensor_scalar(xnt[:], bu[:], tmp1[:, B + 2:B + 3], tmp1[:, B + 6:B + 7], ALU.subtract, ALU.mult)])
                last_xbuf_v = xn_v
                if qtis[i] is not None:
                    cc = 4 * qtis[i]
                    b.vops([
                        lambda e, bu=xbuf, B=B: e.tensor_reduce(tmp1[:, B + 10:B + 11], bu[:], AX, ALU.max),
                        lambda e, bu=xbuf, B=B: e.tensor_reduce(tmp1[:, B + 11:B + 12], bu[:], AX, ALU.min),
                    ])
                    last_xbuf_v = b.c("vX")
                    b.vops([
                        lambda e, B=B: e.tensor_tensor(tmp1[:, B + 10:B + 11], tmp1[:, B + 10:B + 11], tmp1[:, B + 2:B + 3], ALU.subtract),
                        lambda e, B=B: e.tensor_tensor(tmp1[:, B + 11:B + 12], tmp1[:, B + 2:B + 3], tmp1[:, B + 11:B + 12], ALU.subtract),
                        lambda e, B=B: e.tensor_tensor(tmp1[:, B + 3:B + 4], tmp1[:, B + 10:B + 11], tmp1[:, B + 11:B + 12], ALU.max),
                        lambda e, c=cc, B=B: e.tensor_copy(stats[:, c:c + 1], tmp1[:, B + 2:B + 3]),
                        lambda e, c=cc, B=B: e.scalar_tensor_tensor(stats[:, c + 2:c + 3], tmp1[:, B + 3:B + 4], 1.0 / 128.0, tmp1[:, B + 7:B + 8], ALU.mult, ALU.mult),
                        lambda e, B=B: e.reciprocal(tmp1[:, B + 5:B + 6], tmp1[:, B + 3:B + 4]),
                        lambda e, c=cc, B=B: e.tensor_scalar(stats[:, c + 1:c + 2], tmp1[:, B + 5:B + 6], 128.0, None, ALU.mult),
                    ])
                xw[pp] = last_xbuf_v
                b.wait("tensor", "vX", xn_v)
                tth = b.sem("tT")
                vbh = b.sem("vB")
                for jh in range(2):
                    pt = ptrs[jh]
                    b.wait("tensor", "vB", tr_war[jh])
                    for j in range(8):
                        jj = jh * 8 + j
                        if j == 7:
                            b.on("tensor", lambda e, pt=pt, j=j, jj=jj, xnt=xnt, h=tth: e.transpose(pt[:, 128 * j:128 * (j + 1)], xnt[:, 128 * jj:128 * (jj + 1)], ident[:]).then_inc(h, 1))
                        else:
                            b.on("tensor", lambda e, pt=pt, j=j, jj=jj, xnt=xnt: e.transpose(pt[:, 128 * j:128 * (j + 1)], xnt[:, 128 * jj:128 * (jj + 1)], ident[:]))
                    tv = b.inc("tT")
                    b.wait("vector", "tT", tv)
                    dst = xnf_dst_fns[i](jh)
                    b.on("vector", lambda e, pt=pt, d=dst, h=vbh: e.tensor_copy(d, pt[:].rearrange("p (j c) -> p j c", c=128)).then_inc(h, 1))
                    tr_war[jh] = b.inc("vB")
                xnt_war[0] = b.c("tT")

            rst = {}
            for i in range(nT):
                den = stageA(i)
                if i >= 1:
                    stageC(i - 1, rst[i - 1])
                rst[i] = stageB(i, den)
                if per_tile_hook:
                    per_tile_hook()
            stageC(nT - 1, rst[nT - 1])

        p1s = ExitStack()
        gbb = [b.sb(f"gbb{i}", [128, 2048], F32, p1s) for i in range(3)]
        wv_b = b.sb("wv_b", [128, 16 * E], BF16, p1s)
        xt1a = b.sb("xt1a", [128, E], F32, p1s)
        xt1b = b.sb("xt1b", [128, E], F32, p1s)
        xnt1 = b.sb("xnt1", [128, E], BF16, p1s)
        xnfg1 = b.sb("xnfg1", [128, 16 * 1024], BF16, p1s)
        vst = [b.sb("vst0", [128, H * 129], BF16, p1s), b.sb("vst1", [128, H * 129], BF16, p1s)]
        kst = [b.sb("kst0", [128, 512], BF16, p1s), b.sb("kst1", [128, 512], BF16, p1s)]
        ksq = b.sb("ksq", [128, 512], BF16, p1s)
        xts = [xt1a, xt1b]

        for v_ in vst:
            vc = v_[:].rearrange("p (h c) -> p h c", c=129)[:, :, 128:129]
            b.on("gpsimd", lambda e, d=vc: e.memset(d, 1.0).then_inc(b.sem("gA"), 1))
            b.inc("gA")
        gA_vst = b.c("gA")

        wvv = wvT.rearrange("(dc p) o -> p dc o", p=128)
        wv_dst = wv_b[:].rearrange("p (dc o) -> p dc o", o=E)
        wv_rdy = b.dma("gpsimd", "sW", wv_dst, wvv)

        xw = [0, 0]
        pv_war = [0, 0]
        vst_war = [None, None]
        xnf_toks = [None, None]
        b.waitv("tensor", idn_tok)
        for g in range(4):
            def mk_dst1(t8):
                return lambda jh: xnfg1[:].rearrange("p (j c) -> p j c", c=1024)[:, 8 * jh:8 * jh + 8, 128 * t8:128 * (t8 + 1)]
            ln_group([xf[128 * (g * 8 + t8):128 * (g * 8 + t8 + 1), :] for t8 in range(8)],
                     [None] * 8, xts, xnt1, "sX1", xw,
                     [mk_dst1(t8) for t8 in range(8)],
                     [("tC", b.c("tC"))] + [t for t in xnf_toks if t],
                     per_tile_hook=lambda: emit_gamma_chunks(4))
            b.wait("sync", "vB", b.c("vB"))
            for j in range(16):
                xnf_toks[j % 2] = b.dma("sync", "sXN", XNF[j, :, 1024 * g:1024 * (g + 1)], xnfg1[:, 1024 * j:1024 * (j + 1)])
            b.waitv("tensor", wv_rdy)
            b.wait("tensor", "gA", gA_vst)
            b.wait("tensor", "vB", b.c("vB"))
            for t8 in range(8):
                ti = g * 8 + t8
                vstage = vst[ti % 2]
                tch = b.sem("tC")
                vch = b.sem("vC")
                for vh in range(2):
                    b.wait("tensor", "vC", pv_war[vh])
                    for s2 in range(2):
                        for d in range(16):
                            lhs = xnfg1[:, 1024 * d + 128 * t8: 1024 * d + 128 * (t8 + 1)]
                            rhs = wv_b[:, E * d + 1024 * vh + 512 * s2: E * d + 1024 * vh + 512 * (s2 + 1)]
                            pslc = pv[:, 1024 * vh + 512 * s2: 1024 * vh + 512 * (s2 + 1)]
                            if d == 15 and s2 == 1:
                                b.on("tensor", lambda e, p=pslc, l=lhs, r=rhs, h=tch: e.matmul(p, l, r, start=False, stop=True).then_inc(h, 1))
                            else:
                                b.on("tensor", lambda e, p=pslc, l=lhs, r=rhs, d=d: e.matmul(p, l, r, start=(d == 0), stop=(d == 15)))
                    tv = b.inc("tC")
                    b.wait("vector", "tC", tv)
                    b.waitv("vector", vst_war[ti % 2])
                    dstv = vstage[:].rearrange("p (h c) -> p h c", c=129)[:, 8 * vh:8 * vh + 8, 0:128]
                    srcv = pv[:, 1024 * vh:1024 * (vh + 1)].rearrange("p (h c) -> p h c", c=128)
                    b.on("vector", lambda e, d=dstv, s=srcv, h=vch: e.tensor_copy(d, s).then_inc(h, 1))
                    pv_war[vh] = b.inc("vC")
                b.wait("sync", "vC", b.c("vC"))
                vst_war[ti % 2] = b.dma("sync", "sVS", Vs[128 * ti:128 * (ti + 1), :], vstage[:])

        wk_b = wv_b
        b.wait("gpsimd", "tC", b.c("tC"))
        wkk = wkT.rearrange("(dc p) o -> p dc o", p=128)
        wk_rdy = b.dma("gpsimd", "sW", wv_dst, wkk)
        kwar = [0, 0]
        kdma_war = [None, None]
        for g in range(4):
            b.wait("sync", "tC", b.c("tC"))
            b.wait("sync", "tD", b.c("tD"))
            b.wait_all("sync", "sXN")
            for j in range(16):
                b.dma("sync", "sXR", xnfg1[:, 1024 * j:1024 * (j + 1)], XNF[j, :, 1024 * g:1024 * (g + 1)])
            b.wait_all("tensor", "sXR")
            if g == 0:
                b.waitv("tensor", wk_rdy)
            for h_ in range(H):
                for t5 in range(2):
                    idx = g * 32 + h_ * 2 + t5
                    kb = kst[idx % 2]
                    ph = idx % 2
                    b.wait("tensor", "vC", kwar[ph])
                    tdh = b.sem("tD")
                    for d in range(16):
                        lhs = wk_b[:, E * d + 128 * h_: E * d + 128 * (h_ + 1)]
                        rhs = xnfg1[:, 1024 * d + 512 * t5: 1024 * d + 512 * (t5 + 1)]
                        if d == 15:
                            b.on("tensor", lambda e, l=lhs, r=rhs, p=ph, h=tdh: e.matmul(pkh[p], l, r, start=False, stop=True).then_inc(h, 1))
                        else:
                            b.on("tensor", lambda e, l=lhs, r=rhs, p=ph, d=d: e.matmul(pkh[p], l, r, start=(d == 0), stop=False))
                    tv = b.inc("tD")
                    b.wait("vector", "tD", tv)
                    b.waitv("vector", kdma_war[idx % 2])
                    vch = b.sem("vC")
                    b.on("vector", lambda e, kb=kb, p=ph, h=vch: e.tensor_copy(kb[:], pkh[p]).then_inc(h, 1))
                    kev = b.inc("vC")
                    kwar[ph] = kev
                    b.wait("vector", "vC", kev)
                    b.on("vector", lambda e, kb=kb, h=vch: e.tensor_tensor(ksq[:], kb[:], kb[:], ALU.mult).then_inc(h, 1))
                    ksq_v = b.inc("vC")
                    b.wait("tensor", "vC", ksq_v)
                    knh = b.sem("tA")
                    b.on("tensor", lambda e, h=knh: e.matmul(pv[0:1, 0:512], ones_bf[:], ksq[:], start=True, stop=True).then_inc(h, 1))
                    kn_t = b.inc("tA")
                    b.wait("vector", "tA", kn_t)
                    kmv = b.vops([
                        lambda e: e.tensor_reduce(tmp1[0:1, 9:10], pv[0:1, 0:512], AX, ALU.max),
                        lambda e, hh=h_: e.tensor_tensor(kmax2[:, hh:hh + 1], kmax2[:, hh:hh + 1], tmp1[0:1, 9:10], ALU.max),
                    ])
                    b.wait("sync", "vC", kev)
                    kdma_war[idx % 2] = b.dma("sync", "sKS", Ks[h_, :, 1024 * g + 512 * t5: 1024 * g + 512 * (t5 + 1)], kb[:])
        kmax_done_v = b.c("vX")

        assert gk[0] == len(gchunks), "gamma stream must drain inside P1 (gbb dies with p1s)"
        p1s.close()
        b.barrier()

        p23s = ExitStack()
        q_fm = b.sb("q_fm", [128, H * TQ], BF16, p23s)
        p2s = ExitStack()
        wq_b = b.sb("wq_b", [128, 16 * E], BF16, p2s)
        xt2a = b.sb("xt2a", [128, E], F32, p2s)
        xt2b = b.sb("xt2b", [128, E], F32, p2s)
        xnt2 = b.sb("xnt2", [128, E], BF16, p2s)
        xnfg2 = b.sb("xnfg2", [128, 16 * 1024], BF16, p2s)
        qsq = b.sb("qsq", [128, 2048], BF16, p2s)
        xts2 = [xt2a, xt2b]

        wqq = wqT.rearrange("(dc p) o -> p dc o", p=128)
        wq_rdy = b.dma("gpsimd", "sW", wq_b[:].rearrange("p (dc o) -> p dc o", o=E), wqq)

        xw2 = [0, 0]
        qwar = [0, 0]
        for g2 in range(2):
            def mk_dst2(t8):
                return lambda jh: xnfg2[:].rearrange("p (j c) -> p j c", c=1024)[:, 8 * jh:8 * jh + 8, 128 * t8:128 * (t8 + 1)]
            ln_group([xq[128 * (g2 * 8 + t8):128 * (g2 * 8 + t8 + 1), :] for t8 in range(8)],
                     [g2 * 8 + t8 for t8 in range(8)], xts2, xnt2, "sQX", xw2,
                     [mk_dst2(t8) for t8 in range(8)],
                     [("tD", b.c("tD"))], per_tile_hook=lambda: emit_gamma_chunks(2))
            b.wait("tensor", "vB", b.c("vB"))
            if g2 == 0:
                b.waitv("tensor", wq_rdy)
            for h_ in range(H):
                for t5 in range(2):
                    idx = g2 * 32 + h_ * 2 + t5
                    ph = idx % 2
                    b.wait("tensor", "vC", qwar[ph])
                    tdh = b.sem("tD")
                    for d in range(16):
                        lhs = wq_b[:, E * d + 128 * h_: E * d + 128 * (h_ + 1)]
                        rhs = xnfg2[:, 1024 * d + 512 * t5: 1024 * d + 512 * (t5 + 1)]
                        if d == 15:
                            b.on("tensor", lambda e, l=lhs, r=rhs, p=ph, h=tdh: e.matmul(pkh[p], l, r, start=False, stop=True).then_inc(h, 1))
                        else:
                            b.on("tensor", lambda e, l=lhs, r=rhs, p=ph, d=d: e.matmul(pkh[p], l, r, start=(d == 0), stop=False))
                    tv = b.inc("tD")
                    b.wait("vector", "tD", tv)
                    dstq = q_fm[:, TQ * h_ + 1024 * g2 + 512 * t5: TQ * h_ + 1024 * g2 + 512 * (t5 + 1)]
                    vch = b.sem("vC")
                    b.on("vector", lambda e, d=dstq, p=ph, h=vch: e.tensor_copy(d, pkh[p]).then_inc(h, 1))
                    qwar[ph] = b.inc("vC")
        emit_gamma_chunks(len(gchunks))

        b.wait("tensor", "vC", b.c("vC"))
        for h_ in range(H):
            b.wait("vector", "tA", b.c("tA"))
            vch = b.sem("vC")
            b.on("vector", lambda e, h_=h_, h=vch: e.tensor_tensor(qsq[:], q_fm[:, TQ * h_:TQ * h_ + 2048], q_fm[:, TQ * h_:TQ * h_ + 2048], ALU.mult).then_inc(h, 1))
            qv = b.inc("vC")
            b.wait("tensor", "vC", qv)
            for g in range(4):
                b.wait("tensor", "vA", b.c("vA"))
                tah = b.sem("tA")
                b.on("tensor", lambda e, g=g, h=tah: e.matmul(pv[0:1, 0:512], ones_bf[:], qsq[:, 512 * g:512 * (g + 1)], start=True, stop=True).then_inc(h, 1))
                tv = b.inc("tA")
                b.wait("vector", "tA", tv)
                col = 4 * h_ + g
                vah = b.sem("vA")
                b.on("vector", lambda e, c=col, h=vah: e.tensor_reduce(qmax2[0:1, c:c + 1], pv[0:1, 0:512], AX, ALU.max).then_inc(h, 1))
                b.inc("vA")
        b.wait("vector", "vA", b.c("vA"))
        b.wait("vector", "vX", kmax_done_v)
        tops = [(lambda e, h_=h_: e.tensor_scalar(t64[0:1, 4 * h_:4 * h_ + 4], qmax2[0:1, 4 * h_:4 * h_ + 4], kmax2[0:1, h_:h_ + 1], None, ALU.add)) for h_ in range(H)]
        tops.append(lambda e: e.tensor_scalar(tall[:], t64[:], -0.5 * SCALE, None, ALU.mult))
        tall_v = b.vops(tops)
        b.wait("tensor", "vX", tall_v)
        tah = b.sem("tA")
        b.on("tensor", lambda e, h=tah: e.matmul(pv[0:128, 512:576], onesr_bf[:], tall[:], start=True, stop=True).then_inc(h, 1))
        tv = b.inc("tA")
        b.wait("scalar", "tA", tv)
        abh = b.sem("aB")
        b.on("scalar", lambda e, h=abh: e.activation(bias_all[:], pv[0:128, 512:576], AF.Copy).then_inc(h, 1))
        bias_rdy = b.inc("aB")

        b.wait("vector", "aG", b.c("aG"))
        b.vops([
            lambda e: e.tensor_reduce(gacc[:, 1:2], gcols[:, 0:256], AX, ALU.add),
            lambda e: e.tensor_reduce(gacc[:, 0:1], gcols[:, 256:512], AX, ALU.add),
        ])
        b.wait("tensor", "vX", b.c("vX"))
        b.wait("tensor", "vA", b.c("vA"))
        tah = b.sem("tA")
        b.on("tensor", lambda e, h=tah: e.matmul(pv[0:1, 0:2], ones_f32[:], gacc[:], start=True, stop=True).then_inc(h, 1))
        gm_v = b.inc("tA")
        b.wait("scalar", "tA", gm_v)
        abh = b.sem("aB")
        b.on("scalar", lambda e, h=abh: e.activation(sc0[:, 0:2], pv[0:1, 0:2], AF.Copy).then_inc(h, 1))
        b.inc("aB")
        b.wait("vector", "aB", b.c("aB"))
        gfin = b.vops([
            lambda e: e.tensor_scalar(sc0[:, 2:4], sc0[:, 0:2], 1.0 / (E * FF), None, ALU.mult),
            lambda e: e.tensor_scalar(sc0[:, 4:6], sc0[:, 2:4], 1e-6, None, ALU.add),
            lambda e: e.tensor_scalar(sc0[:, 4:6], sc0[:, 4:6], 0.5, None, ALU.mult),
            lambda e: e.tensor_scalar(sc0[:, 6:8], sc0[:, 4:6], -1.0, None, ALU.mult),
        ])
        b.wait("tensor", "vX", gfin)
        b.wait("tensor", "aB", b.c("aB"))
        tah2 = b.sem("tA")
        b.on("tensor", lambda e, h=tah2: e.matmul(pv[0:128, 512:518], onesr_f32[:], sc0[:, 2:8], start=True, stop=True).then_inc(h, 1))
        gbt = b.inc("tA")
        b.wait("scalar", "tA", gbt)
        sgbh = b.sem("sGB")
        b.on("scalar", lambda e, h=sgbh: e.activation(gall[:], pv[0:128, 512:518], AF.Copy).then_inc(h, 1))
        gb_rdy = b.inc("sGB")
        if debug:
            b.wait("sync", "vC", b.c("vC"))
            b.dma("sync", "sDB", QDBG, q_fm[:])
            b.wait("sync", "sGB", gb_rdy)
            b.dma("sync", "sDB", GDBG[:, 0:6], gall[:])
            b.dma("sync", "sDB", GDBG[:, 6:8], gacc[:])
            b.dma("sync", "sDB", GDBG[0:1, 8:16], sc0[:, 0:8])

        p2s.close()
        b.barrier()

        p3os = ExitStack()
        p3s = ExitStack()
        kbuf = [b.sb("kb0", [128, S], BF16, p3s), b.sb("kb1", [128, S], BF16, p3s)]
        vbuf = [b.sb("vb0", [128, 32 * 258], BF16, p3s), b.sb("vb1", [128, 32 * 258], BF16, p3s)]
        att = [b.sb(f"att{i}", [128, 512], BF16, p3s) for i in range(4)]
        cst = [b.sb("cst0", [128, 512], BF16, p3s), b.sb("cst1", [128, 512], BF16, p3s)]
        masks = b.sb("masks", [128, 8 * 512], BF16, p3s)
        pjst = [b.sb(f"pjst{i}", [128, 512], F32, p3s) for i in range(3)]
        _gcb = gcols.bitcast(BF16).ap()
        pjtn = [_gcb[:, 0:512], _gcb[:, 512:1024]]
        rlb = b.sb("rlb", [128, 4], F32, p3s)
        rlc = b.sb("rlc", [128, 4], F32, p3s)

        pm = ptm.rearrange("r p c -> p r c")
        mk_rdy = b.dma("sync", "sIN", masks[:].rearrange("p (r c) -> p r c", c=512), pm)

        pjk = [0]
        tw_toks = [None, None]
        wpv = wpjT.rearrange("(n p) c -> n p c", p=128)

        pj_lv = {}
        pj_cv = [0, 0, 0]

        def pj_issue(kk):
            if kk >= 256:
                return
            bufA = pjst[kk % 3]
            if kk >= 3:
                b.wait("gpsimd", "vX", pj_cv[kk % 3])
            src = wpv[kk // 4, :, 512 * (kk % 4):512 * (kk % 4) + 512]
            pj_lv[kk] = [b.dma("gpsimd", "sPJ", bufA[:, 256 * hh:256 * (hh + 1)],
                               src[:, 256 * hh:256 * (hh + 1)]) for hh in range(2)]

        def emit_pj_chunk(n=1):
            for _ in range(n):
                kk = pjk[0]
                if kk >= 256:
                    return
                if kk == 0:
                    b.wait("vector", "sGB", gb_rdy)
                    for j in range(2):
                        pj_issue(j)
                bufA = pjst[kk % 3]
                bufB = pjtn[kk % 2]
                if kk >= 2:
                    b.waitv("vector", tw_toks[kk % 2])
                for t_ in pj_lv[kk]:
                    b.waitv("vector", t_)
                tv = b.vops([
                    lambda e, a=bufA, t=bufB: e.tensor_single_scalar(t, a[:], thpj_b, ALU.is_gt),
                    lambda e, a=bufA, t=bufB: e.scalar_tensor_tensor(t, a[:], nthpj_b, t, ALU.is_ge, ALU.add),
                ])
                pj_cv[kk % 3] = tv
                b.wait("gpsimd", "vX", tv)
                tw_toks[kk % 2] = b.dma("gpsimd", "sTW", TWPJ[128 * (kk // 4):128 * (kk // 4 + 1), 512 * (kk % 4):512 * (kk % 4) + 512], bufB)
                pj_issue(kk + 2)
                pjk[0] += 1

        kh_rdy = [b.dma("sync", "sKH", kbuf[0][:], Ks[0, :, :]),
                  b.dma("sync", "sKH", kbuf[1][:], Ks[1, :, :])]
        vsr = Vs.rearrange("(k p) c -> p k c", p=128)
        vp_rdy = [b.dma("sync", "sVP", vbuf[0][:].rearrange("p (k c) -> p k c", c=258), vsr[:, :, 0:258]),
                  b.dma("sync", "sVP", vbuf[1][:].rearrange("p (k c) -> p k c", c=258), vsr[:, :, 258:516])]

        ctx_slot_war = [0]
        cst_war = [None, None]
        ts_after_head = []
        tx_after_head = []
        for h_ in range(H):
            b.waitv("tensor", kh_rdy[h_])
            b.waitv("tensor", vp_rdy[h_ // 2])
            if h_ == 0:
                b.wait("scalar", "aB", bias_rdy)
                b.waitv("vector", mk_rdy)
                b.waitv("scalar", mk_rdy)
            kb = kbuf[h_ % 2]
            vb_ = vbuf[(h_ // 2) % 2]
            vslice = vb_[:].rearrange("p (k c) -> p k c", c=258)
            voff = 129 * (h_ % 2)
            sbanks = (pkh[0], pkh[1], pbf[0], pbf[1])
            for g in range(4):
                Eg = 8 * g + 8
                cstg = cst[g % 2]
                expbase = b.c("aC")
                ctxbase = b.c("tX")
                bcol = 4 * h_ + g
                tsh = b.sem("tS")
                ach = b.sem("aC")
                txh = b.sem("tX")
                PD = 3
                exp_vs, msk_vs = {}, {}

                def _ilo(kt):
                    r = kt - 8 * g
                    return (r // 2 if r >= 0 else 0), r

                for pos in range(Eg + PD):
                    if pos < Eg:
                        kt = pos
                        ph = kt % 4
                        ilo, r = _ilo(kt)
                        co = 128 * ilo
                        b.wait("tensor", "aC", expbase + kt - 3)
                        b.on("tensor", lambda e, kb=kb, kt=kt, p=ph, co=co, h=tsh, hh=h_, g=g: e.matmul(
                            sbanks[p][:, co:512], kb[:, 128 * kt:128 * (kt + 1)],
                            q_fm[:, TQ * hh + 512 * g + co: TQ * hh + 512 * (g + 1)],
                            start=True, stop=True).then_inc(h, 1))
                        st_v = b.inc("tS")
                        b.wait("scalar", "tS", st_v)
                        b.wait("scalar", "tX", ctxbase + kt - 3)
                        b.on("scalar", lambda e, p=ph, co=co, a=att[ph], c=bcol, h=ach: e.activation(
                            a[:, co:512], sbanks[p][:, co:512], AF.Exp, bias=bias_all[:, c:c + 1], scale=SCALE).then_inc(h, 1))
                        exp_vs[kt] = b.inc("aC")
                        if r >= 0:
                            b.wait("vector", "aC", exp_vs[kt])
                            vmh = b.sem("vM")
                            b.on("vector", lambda e, a=att[ph], r=r, co=co, h=vmh: e.tensor_tensor(a[:, co:512], a[:, co:512], masks[:, 512 * r + co:512 * (r + 1)], ALU.mult).then_inc(h, 1))
                            msk_vs[kt] = b.inc("vM")
                    kd = pos - PD
                    if 0 <= kd < Eg:
                        ph = kd % 4
                        ilo, r = _ilo(kd)
                        if r >= 0:
                            b.wait("tensor", "vM", msk_vs[kd])
                        else:
                            b.wait("tensor", "aC", exp_vs[kd])
                        if kd == 0:
                            b.wait("tensor", "vD", ctx_slot_war[0])
                        slots = list(range(ilo, 4))
                        for i in slots:
                            pslc = pv[:, 512 * i:512 * i + 129]
                            lhs = att[ph][:, 128 * i:128 * (i + 1)]
                            rhs = vslice[:, kd, voff:voff + 129]
                            stop_i = (kd == 8 * g + 2 * i + 1)
                            if i == slots[-1]:
                                b.on("tensor", lambda e, p=pslc, l=lhs, r=rhs, kd=kd, st=stop_i, h=txh: e.matmul(p, l, r, start=(kd == 0), stop=st).then_inc(h, 1))
                            else:
                                b.on("tensor", lambda e, p=pslc, l=lhs, r=rhs, kd=kd, st=stop_i: e.matmul(p, l, r, start=(kd == 0), stop=st))
                        b.inc("tX")
                        ie = (kd - 8 * g - 1) // 2
                        if kd >= 8 * g + 1 and kd == 8 * g + 2 * ie + 1 and 0 <= ie < 4:
                            b.wait("vector", "tX", b.c("tX"))
                            if ie == 0:
                                b.waitv("vector", cst_war[g % 2])
                            vdh2 = b.sem("vD")
                            rl_v = b.vops([
                                lambda e, i=ie: e.tensor_scalar(rlb[:, i:i + 1], pv[:, 512 * i + 128:512 * i + 129], GUARD, None, ALU.add),
                                lambda e, i=ie: e.reciprocal(rlc[:, i:i + 1], rlb[:, i:i + 1]),
                            ])
                            b.wait("vector", "vX", rl_v)
                            b.on("vector", lambda e, i=ie, c=cstg, h=vdh2: e.tensor_scalar(c[:, 128 * i:128 * (i + 1)], pv[:, 512 * i:512 * i + 128], rlc[:, i:i + 1], None, ALU.mult).then_inc(h, 1))
                            b.inc("vD")
                ctx_slot_war[0] = b.c("vD")
                b.wait("gpsimd", "vD", ctx_slot_war[0])
                dstc = CTX[h_, 512 * g:512 * (g + 1), :].rearrange("(i p) c -> p i c", p=128)
                cst_war[g % 2] = b.dma("gpsimd", "sCW", dstc, cstg[:].rearrange("p (i c) -> p i c", c=128))
                emit_pj_chunk(4)
            ts_after_head.append(b.c("tS"))
            tx_after_head.append(b.c("tX"))
            if h_ + 2 < H:
                b.wait("sync", "tS", ts_after_head[h_])
                kh_rdy.append(b.dma("sync", "sKH", kbuf[h_ % 2][:], Ks[h_ + 2, :, :]))
            if h_ % 2 == 1 and (h_ + 3) // 2 < 8:
                p_ = (h_ + 3) // 2
                b.wait("sync", "tX", tx_after_head[h_])
                vp_rdy.append(b.dma("sync", "sVP", vbuf[p_ % 2][:].rearrange("p (k c) -> p k c", c=258), vsr[:, :, 258 * p_:258 * (p_ + 1)]))
        emit_pj_chunk(256)

        p3s.close()
        b.barrier()

        p35s = ExitStack()
        wot_b = b.sb("wot_b", [128, 16 * E], BF16, p35s)
        woo = woT.rearrange("(dc p) o -> p dc o", p=128)
        wo_rdy = b.dma("gpsimd", "sWO", wot_b[:].rearrange("p (dc o) -> p dc o", o=E), woo)
        bo_b = b.sb("bo_b", [128, E], F32, p35s)
        cstm = [b.sb("cstm0", [128, 2048], BF16, p35s), b.sb("cstm1", [128, 2048], BF16, p35s)]
        cfm = b.sb("cfm", [128, 2048], BF16, p35s)
        xqs = [b.sb("xqs0", [128, E], F32, p35s), b.sb("xqs1", [128, E], F32, p35s)]
        aos = [b.sb("aos0", [128, E], F32, p35s), b.sb("aos1", [128, E], F32, p35s)]
        bo_rdy = b.dma("sync", "sIN", bo_b[:], bo.partition_broadcast(128))
        b.waitv("tensor", wo_rdy)
        b.wait_all("sync", "sCW")
        tr35 = [b.c("vB"), b.c("vB")]
        pv35 = [b.c("vD"), b.c("vD")]
        cfm_war = [0]
        aos_war = [None, None]
        xq_war = [0, 0]
        tt_tT = [0] * 16
        for tt in range(16):
            cbuf = cstm[tt % 2]
            if tt >= 2:
                b.wait("sync", "tT", tt_tT[tt - 2])
            lv = 0
            for h_ in range(H):
                lv = b.dma("sync", "sCR", cbuf[:, 128 * h_:128 * (h_ + 1)], CTX[h_, 128 * tt:128 * (tt + 1), :])
            b.wait("sync", "vD", xq_war[tt % 2])
            xb = xqs[tt % 2]
            xv = b.dma("sync", "sAQ", xb[:], xq[128 * tt:128 * (tt + 1), :])
            b.waitv("tensor", lv)
            b.wait("tensor", "tP", b.c("tP"))
            tth = b.sem("tT")
            vbh = b.sem("vB")
            for jh in range(2):
                pt = ptrs[jh]
                b.wait("tensor", "vB", tr35[jh])
                for j in range(8):
                    jj = jh * 8 + j
                    if j == 7:
                        b.on("tensor", lambda e, pt=pt, j=j, jj=jj, c=cbuf, h=tth: e.transpose(pt[:, 128 * j:128 * (j + 1)], c[:, 128 * jj:128 * (jj + 1)], ident[:]).then_inc(h, 1))
                    else:
                        b.on("tensor", lambda e, pt=pt, j=j, jj=jj, c=cbuf: e.transpose(pt[:, 128 * j:128 * (j + 1)], c[:, 128 * jj:128 * (jj + 1)], ident[:]))
                tv = b.inc("tT")
                b.wait("vector", "tT", tv)
                dst = cfm[:].rearrange("p (j c) -> p j c", c=128)[:, 8 * jh:8 * jh + 8, :]
                b.on("vector", lambda e, pt=pt, d=dst, h=vbh: e.tensor_copy(d, pt[:].rearrange("p (j c) -> p j c", c=128)).then_inc(h, 1))
                tr35[jh] = b.inc("vB")
            tt_tT[tt] = b.c("tT")
            b.wait("tensor", "vB", b.c("vB"))
            for eh in range(2):
                b.wait("tensor", "vD", pv35[eh])
                tph = b.sem("tP")
                for dh in range(16):
                    for s2 in range(2):
                        lhs = cfm[:, 128 * dh:128 * (dh + 1)]
                        rhs = wot_b[:, E * dh + 1024 * eh + 512 * s2: E * dh + 1024 * eh + 512 * (s2 + 1)]
                        pslc = pv[:, 1024 * eh + 512 * s2: 1024 * eh + 512 * (s2 + 1)]
                        if dh == 15 and s2 == 1:
                            b.on("tensor", lambda e, p=pslc, l=lhs, r=rhs, h=tph: e.matmul(p, l, r, start=False, stop=True).then_inc(h, 1))
                        else:
                            b.on("tensor", lambda e, p=pslc, l=lhs, r=rhs, dh=dh, s2=s2: e.matmul(p, l, r, start=(dh == 0), stop=(dh == 15 and s2 == 1)))
                tv = b.inc("tP")
                b.wait("vector", "tP", tv)
                if tt == 0 and eh == 0:
                    b.waitv("vector", bo_rdy)
                b.waitv("vector", xv)
                b.waitv("vector", aos_war[tt % 2])
                ab = aos[tt % 2]
                vdh = b.sem("vD")
                b.on("vector", lambda e, a=ab, eh=eh, h=vdh: e.tensor_tensor(a[:, 1024 * eh:1024 * (eh + 1)], pv[:, 1024 * eh:1024 * (eh + 1)], bo_b[:, 1024 * eh:1024 * (eh + 1)], ALU.add).then_inc(h, 1))
                b.inc("vD")
                b.on("vector", lambda e, a=ab, x=xb, eh=eh, h=vdh: e.tensor_tensor(a[:, 1024 * eh:1024 * (eh + 1)], a[:, 1024 * eh:1024 * (eh + 1)], x[:, 1024 * eh:1024 * (eh + 1)], ALU.add).then_inc(h, 1))
                pv35[eh] = b.inc("vD")
            xq_war[tt % 2] = b.c("vD")
            b.wait("sync", "vD", b.c("vD"))
            aos_war[tt % 2] = b.dma("sync", "sAW", AOUT[128 * tt:128 * (tt + 1), :], aos[tt % 2][:])

        p35s.close()
        p3os.close()
        p23s.close()
        b.barrier()

        p4s = ExitStack()
        x1fm = b.sb("x1fm", [128, 16 * TQ], BF16, p4s)
        xt4a = b.sb("xt4a", [128, E], F32, p4s)
        xt4b = b.sb("xt4b", [128, E], F32, p4s)
        x1t = b.sb("x1t", [128, E], BF16, p4s)
        twfc = [b.sb("twfc0", [128, 16 * 512], BF16, p4s), b.sb("twfc1", [128, 16 * 512], BF16, p4s)]
        fcs = [b.sb("fcs0", [128, 512], F32, p4s), b.sb("fcs1", [128, 512], F32, p4s)]
        gls = [b.sb("gls0", [128, 512], F32, p4s), b.sb("gls1", [128, 512], F32, p4s)]
        xts4 = [xt4a, xt4b]

        b.wait("vector", "sGB", gb_rdy)
        tr4 = [b.c("vB"), b.c("vB")]
        xw4 = [0, 0]
        x1t_war = [0]
        for tt in range(16):
            xb = xts4[tt % 2]
            b.wait("sync", "vX", xw4[tt % 2])
            lv = b.dma("sync", "sQX", xb[:], xq[128 * tt:128 * (tt + 1), :])
            b.waitv("vector", lv)
            b.wait("vector", "tT", x1t_war[0])
            cc = 4 * tt
            x1_v = b.vops([
                lambda e, c=cc: e.tensor_tensor(tmp1[:, 0:1], stats[:, c + 2:c + 3], gfc_b, ALU.mult),
                lambda e, c=cc: e.tensor_tensor(tmp1[:, 1:2], stats[:, c + 1:c + 2], tmp1[:, 0:1], ALU.mult),
                lambda e: e.tensor_scalar(tmp1[:, 2:3], tmp1[:, 0:1], CLIPB, None, ALU.mult),
                lambda e: e.tensor_scalar(tmp1[:, 3:4], tmp1[:, 2:3], -1.0, None, ALU.mult),
                lambda e, xb=xb, c=cc: e.tensor_scalar(x1t[:], xb[:], stats[:, c:c + 1], tmp1[:, 1:2], ALU.subtract, ALU.mult),
                lambda e: e.tensor_scalar(x1t[:], x1t[:], tmp1[:, 2:3], None, ALU.min),
                lambda e: e.tensor_scalar(x1t[:], x1t[:], tmp1[:, 3:4], None, ALU.max, ALU.add, accum_out=tmp1[:, 4:5]),
            ])
            xw4[tt % 2] = x1_v
            b.vops([
                lambda e, c=cc: e.tensor_scalar(stats[:, c + 3:c + 4], tmp1[:, 4:5], -1.0, None, ALU.mult),
            ])
            b.wait("tensor", "vX", x1_v)
            tth = b.sem("tT")
            vbh = b.sem("vB")
            for jh in range(2):
                pt = ptrs[jh]
                b.wait("tensor", "vB", tr4[jh])
                for j in range(8):
                    jj = jh * 8 + j
                    if j == 7:
                        b.on("tensor", lambda e, pt=pt, j=j, jj=jj, h=tth: e.transpose(pt[:, 128 * j:128 * (j + 1)], x1t[:, 128 * jj:128 * (jj + 1)], ident[:]).then_inc(h, 1))
                    else:
                        b.on("tensor", lambda e, pt=pt, j=j, jj=jj: e.transpose(pt[:, 128 * j:128 * (j + 1)], x1t[:, 128 * jj:128 * (jj + 1)], ident[:]))
                tv = b.inc("tT")
                b.wait("vector", "tT", tv)
                dst = x1fm[:].rearrange("p (j c) -> p j c", c=TQ)[:, 8 * jh:8 * jh + 8, 128 * tt:128 * (tt + 1)]
                b.on("vector", lambda e, pt=pt, d=dst, h=vbh: e.tensor_copy(d, pt[:].rearrange("p (j c) -> p j c", c=128)).then_inc(h, 1))
                tr4[jh] = b.inc("vB")
            x1t_war[0] = b.c("tT")
        b.wait("tensor", "vB", b.c("vB"))

        pvq_war = [b.c("aB")] * 4
        fcg_war = [0, 0]
        fc_toks = [0, 0]
        gls_war = [None, None]
        for oc in range(16):
            tw = twfc[oc % 2]
            for d in range(16):
                it = oc * 16 + d
                fb = fcs[it % 2]
                if it >= 2:
                    b.wait("gpsimd", "vX", fc_toks[it % 2])
                lv = b.dma("gpsimd", "sFC", fb[:], wfcT[128 * d:128 * (d + 1), 512 * oc:512 * (oc + 1)])
                b.waitv("vector", lv)
                b.wait("vector", "tG", fcg_war[oc % 2])
                dstt = tw[:, 512 * d:512 * (d + 1)]
                fc_toks[it % 2] = b.vops([
                    lambda e, a=fb, t=dstt: e.tensor_single_scalar(t, a[:], thfc_b, ALU.is_gt),
                    lambda e, a=fb, t=dstt: e.scalar_tensor_tensor(t, a[:], nthfc_b, t, ALU.is_ge, ALU.add),
                ])
            b.wait("tensor", "vX", b.c("vX"))
            for tt in range(16):
                it = oc * 16 + tt
                qd = it % 4
                b.wait("tensor", "aB", pvq_war[qd])
                tgh = b.sem("tG")
                for d in range(16):
                    lhs = x1fm[:, TQ * d + 128 * tt: TQ * d + 128 * (tt + 1)]
                    rhs = tw[:, 512 * d:512 * (d + 1)]
                    pslc = pv[:, 512 * qd:512 * (qd + 1)]
                    if d == 15:
                        b.on("tensor", lambda e, p=pslc, l=lhs, r=rhs, h=tgh: e.matmul(p, l, r, start=False, stop=True).then_inc(h, 1))
                    else:
                        b.on("tensor", lambda e, p=pslc, l=lhs, r=rhs, d=d: e.matmul(p, l, r, start=(d == 0), stop=False))
                tv = b.inc("tG")
                b.wait("scalar", "tG", tv)
                gb_ = gls[it % 2]
                b.waitv("scalar", gls_war[it % 2])
                abh = b.sem("aB")
                b.on("scalar", lambda e, g=gb_, qd=qd, c=4 * tt + 3, h=abh: e.activation(g[:], pv[:, 512 * qd:512 * (qd + 1)], AF.Gelu, bias=stats[:, c:c + 1]).then_inc(h, 1))
                pvq_war[qd] = b.inc("aB")
                b.wait("sync", "aB", pvq_war[qd])
                gls_war[it % 2] = b.dma("sync", "sHW", Hs[128 * tt:128 * (tt + 1), 512 * oc:512 * (oc + 1)], gb_[:])
            fcg_war[oc % 2] = b.c("tG")

        p4s.close()
        b.barrier()

        p5s = ExitStack()
        hst = b.sb("hst", [128, FF], F32, p5s)
        X2T = [b.sb(f"x2t{i}", [128, FF], BF16, p5s) for i in range(4)]
        x2fm = b.sb("x2fm", [128, 64 * 512], BF16, p5s)
        NR = 16
        ring = [b.sb(f"ring{i}", [128, 512], BF16, p5s)[:] for i in range(NR)]
        aosb = [b.sb(f"aq{i}", [128, 512], F32, p5s) for i in range(4)]
        ostb = [b.sb(f"oq{i}", [128, 512], F32, p5s) for i in range(4)]
        pvq = [pv[:, 512 * i:512 * (i + 1)] for i in range(4)]

        b.wait_all("sync", "sTW")
        b.wait_all("gpsimd", "sTW")
        b.wait_all("scalar", "sHW")

        ring_war = [0] * NR
        ring_tok = {}
        evac_aB = [0] * 4
        hst_war = [0]
        x2t_rdy = [0] * 4
        x2t_war = [0] * 4
        st2_v = [0] * 16
        tr_aP = [0, 0]
        ostb_war = [None] * 4
        aosb_war = [0] * 4
        kfin = [0]

        def pf_ring(fi):
            if fi >= 256:
                return
            och4 = (fi // 16) % 4
            og = fi % 16
            eng, nm = ("sync", "sTR") if og % 2 == 0 else ("gpsimd", "sTG")
            pairs = []
            for oo in range(4):
                o = 4 * og + oo
                b.wait(eng, "tF", ring_war[(4 * fi + oo) % NR])
                pairs.append((ring[(4 * fi + oo) % NR],
                              TWPJ[128 * o:128 * (o + 1), 512 * och4:512 * (och4 + 1)]))
            ring_tok[fi] = b.dma_group(eng, nm, pairs, nring=4)

        def emit_subB(tt):
            tt4 = tt % 4
            xt = X2T[tt4]
            b.wait("scalar", "vX", hst_war[0])
            toks = [b.dma_group("scalar", "sHR", [
                (hst[:, 1024 * hf:1024 * (hf + 1)],
                 Hs[128 * tt:128 * (tt + 1), 1024 * hf:1024 * (hf + 1)])
                for hf in range(4 * gg, 4 * gg + 4)], nring=4) for gg in range(2)]
            for t in toks:
                b.waitv("vector", t)
            b.wait("vector", "vX", b.c("vX"))
            vxh = b.sem("vX")
            for g in range(15):
                b.on("vector", lambda e, g=g: e.bn_stats(bnst[:, 6 * g:6 * g + 6], hst[:, 512 * g:512 * (g + 1)]))
            b.on("vector", lambda e, h=vxh: e.bn_stats(bnst[:, 90:96], hst[:, 7680:8192]).then_inc(h, 1))
            b.inc("vX")
            den_v = b.vops([
                lambda e: e.bn_aggr(bnag[:, 0:2], bnst[:, 0:48]),
                lambda e: e.bn_aggr(bnag[:, 2:4], bnst[:, 48:96]),
                lambda e: e.tensor_tensor(tmp1[:, 0:1], bnag[:, 0:1], bnag[:, 2:3], ALU.add),
                lambda e: e.tensor_scalar(tmp1[:, 0:1], tmp1[:, 0:1], 0.5, None, ALU.mult),
                lambda e: e.tensor_tensor(tmp1[:, 1:2], bnag[:, 0:1], bnag[:, 2:3], ALU.subtract),
                lambda e: e.tensor_tensor(tmp1[:, 1:2], tmp1[:, 1:2], tmp1[:, 1:2], ALU.mult),
                lambda e: e.tensor_tensor(tmp1[:, 2:3], bnag[:, 1:2], bnag[:, 3:4], ALU.add),
                lambda e: e.tensor_scalar(tmp1[:, 2:3], tmp1[:, 2:3], 0.5, None, ALU.mult),
                lambda e: e.tensor_scalar(tmp1[:, 1:2], tmp1[:, 1:2], 0.25, None, ALU.mult),
                lambda e: e.tensor_tensor(tmp1[:, 2:3], tmp1[:, 2:3], tmp1[:, 1:2], ALU.add),
                lambda e: e.tensor_reduce(tmp1[:, 4:5], hst[:, 0:4096], AX, ALU.max),
                lambda e: e.tensor_reduce(tmp1[:, 5:6], hst[:, 4096:8192], AX, ALU.max),
                lambda e: e.tensor_reduce(tmp1[:, 6:7], hst[:, 0:4096], AX, ALU.min),
                lambda e: e.tensor_reduce(tmp1[:, 7:8], hst[:, 4096:8192], AX, ALU.min),
                lambda e: e.tensor_tensor(tmp1[:, 4:5], tmp1[:, 4:5], tmp1[:, 5:6], ALU.max),
                lambda e: e.tensor_tensor(tmp1[:, 6:7], tmp1[:, 6:7], tmp1[:, 7:8], ALU.min),
                lambda e: e.tensor_scalar(tmp1[:, 3:4], tmp1[:, 2:3], 1e-6, None, ALU.add),
            ])
            b.wait("scalar", "vX", den_v)
            r2_v = b.vops([
                lambda e: e.activation(tmp1[:, 8:9], tmp1[:, 3:4], AF.Ln),
                lambda e: e.activation(tmp1[:, 9:10], tmp1[:, 8:9], AF.Exp, scale=-0.5),
            ], eng="scalar")
            b.wait("vector", "aX", r2_v)
            b.wait("vector", "tT", x2t_war[tt4])
            x2_v = b.vops([
                lambda e: e.tensor_tensor(tmp1[:, 4:5], tmp1[:, 4:5], tmp1[:, 0:1], ALU.subtract),
                lambda e: e.tensor_tensor(tmp1[:, 6:7], tmp1[:, 0:1], tmp1[:, 6:7], ALU.subtract),
                lambda e: e.tensor_tensor(tmp1[:, 4:5], tmp1[:, 4:5], tmp1[:, 6:7], ALU.max),
                lambda e: e.tensor_tensor(tmp1[:, 11:12], tmp1[:, 9:10], gpj_b, ALU.mult),
                lambda e: e.scalar_tensor_tensor(tmp1[:, 10:11], tmp1[:, 4:5], 1.0 / 128.0, tmp1[:, 9:10], ALU.mult, ALU.mult),
                lambda e: e.tensor_tensor(tmp1[:, 10:11], tmp1[:, 10:11], gpj_b, ALU.mult),
                lambda e: e.tensor_scalar(tmp1[:, 12:13], tmp1[:, 10:11], CLIPB, None, ALU.mult),
                lambda e: e.tensor_scalar(tmp1[:, 13:14], tmp1[:, 10:11], -CLIPB, None, ALU.mult),
                lambda e, xt=xt: e.tensor_scalar(xt[:], hst[:], tmp1[:, 0:1], tmp1[:, 11:12], ALU.subtract, ALU.mult),
                lambda e, xt=xt: e.tensor_scalar(xt[:], xt[:], tmp1[:, 12:13], None, ALU.min),
                lambda e, xt=xt: e.tensor_scalar(xt[:], xt[:], tmp1[:, 13:14], None, ALU.max, ALU.add, accum_out=tmp1[:, 14:15]),
            ])
            hst_war[0] = x2_v
            st2_v[tt] = b.vops([
                lambda e, c=4 * tt: e.tensor_scalar(stats2[:, c + 3:c + 4], tmp1[:, 14:15], -1.0, None, ALU.mult),
            ])
            x2t_rdy[tt4] = x2_v

        def emit_transposes(q):
            for tt4 in range(4):
                b.wait("tensor", "vX", x2t_rdy[tt4])
                xt = X2T[tt4]
                for j8 in range(8):
                    pt = ptrs[j8 % 2]
                    b.wait("tensor", "aP", tr_aP[j8 % 2])
                    tth = b.sem("tT")
                    for j in range(8):
                        jj = j8 * 8 + j
                        if j == 7:
                            b.on("tensor", lambda e, pt=pt, j=j, jj=jj, xt=xt, h=tth: e.transpose(pt[:, 128 * j:128 * (j + 1)], xt[:, 128 * jj:128 * (jj + 1)], ident[:]).then_inc(h, 1))
                        else:
                            b.on("tensor", lambda e, pt=pt, j=j, jj=jj, xt=xt: e.transpose(pt[:, 128 * j:128 * (j + 1)], xt[:, 128 * jj:128 * (jj + 1)], ident[:]))
                    tv = b.inc("tT")
                    b.wait("scalar", "tT", tv)
                    aph = b.sem("aP")
                    dst = x2fm[:].rearrange("p (o c) -> p o c", c=512)[:, 8 * j8:8 * j8 + 8, 128 * tt4:128 * (tt4 + 1)]
                    b.on("scalar", lambda e, pt=pt, d=dst, h=aph: e.activation(d, pt[:].rearrange("p (j c) -> p j c", c=128), AF.Copy).then_inc(h, 1))
                    tr_aP[j8 % 2] = b.inc("aP")
                x2t_war[tt4] = b.c("tT")

        def emit_og_pass(q, och4):
            p = 4 * q + och4
            base_fi = 16 * p
            if p == 0:
                for k in range(4):
                    pf_ring(k)
            b.wait("tensor", "aP", b.c("aP"))
            for og in range(16):
                fi = base_fi + og
                b.waitv("tensor", ring_tok[fi])
                tfh = b.sem("tF")
                nmm = 0
                for tt4 in range(4):
                    if og == 0:
                        b.wait("tensor", "aB", evac_aB[tt4])
                    for oo in range(4):
                        o = 4 * og + oo
                        lhs = x2fm[:, 512 * o + 128 * tt4: 512 * o + 128 * (tt4 + 1)]
                        rhs = ring[(4 * fi + oo) % NR]
                        st = (og == 0 and oo == 0)
                        sp = (og == 15 and oo == 3)
                        nmm += 1
                        if nmm == 16:
                            b.on("tensor", lambda e, pq=pvq[tt4], l=lhs, r=rhs, st=st, sp=sp, h=tfh: e.matmul(pq, l, r, start=st, stop=sp).then_inc(h, 1))
                        else:
                            b.on("tensor", lambda e, pq=pvq[tt4], l=lhs, r=rhs, st=st, sp=sp: e.matmul(pq, l, r, start=st, stop=sp))
                tv = b.inc("tF")
                for oo in range(4):
                    ring_war[(4 * fi + oo) % NR] = tv
                pf_ring(fi + 4)

        def emit_finals(q, och4):
            b.wait("scalar", "tF", b.c("tF"))
            for tt4 in range(4):
                tt = 4 * q + tt4
                k = kfin[0]
                kfin[0] += 1
                ob = ostb[k % 4]
                a = k % 4
                b.wait("scalar", "vD", aosb_war[a])
                atok = b.dma("scalar", "sAR", aosb[a][:], AOUT[128 * tt:128 * (tt + 1), 512 * och4:512 * (och4 + 1)])
                b.waitv("scalar", ostb_war[k % 4])
                b.wait("scalar", "vX", st2_v[tt])
                abh = b.sem("aB")
                b.on("scalar", lambda e, ob=ob, pq=pvq[tt4], c=4 * tt + 3, h=abh: e.activation(ob[:], pq, AF.Identity, bias=stats2[:, c:c + 1]).then_inc(h, 1))
                evac_aB[tt4] = b.inc("aB")
                b.wait("vector", "aB", evac_aB[tt4])
                b.waitv("vector", atok)
                vdh = b.sem("vD")
                b.on("vector", lambda e, ob=ob, a=a, h=vdh: e.tensor_tensor(ob[:], ob[:], aosb[a][:], ALU.add).then_inc(h, 1))
                ov = b.inc("vD")
                aosb_war[a] = ov
                b.wait("sync", "vD", ov)
                ostb_war[k % 4] = b.dma("sync", "sOU", out[128 * tt:128 * (tt + 1), 512 * och4:512 * (och4 + 1)], ob[:])

        for tt in range(4):
            emit_subB(tt)
        emit_transposes(0)
        for q in range(4):
            for och4 in range(4):
                if q + 1 < 4:
                    emit_subB(4 * (q + 1) + och4)
                emit_og_pass(q, och4)
                if och4 == 3 and q + 1 < 4:
                    emit_transposes(q + 1)
                emit_finals(q, och4)

        p5s.close()
        b.wait_all("sync", "sOU")

        with nc.Block() as block:
            @block.sync
            def _(e):
                for f in b.q["sync"]:
                    f(e)

            @block.gpsimd
            def _(e):
                for f in b.q["gpsimd"]:
                    f(e)

            @block.vector
            def _(e):
                for f in b.q["vector"]:
                    f(e)

            @block.scalar
            def _(e):
                for f in b.q["scalar"]:
                    f(e)

            @block.tensor
            def _(e):
                for f in b.q["tensor"]:
                    f(e)

    return nc


def _make_masks(half):
    pt = np.zeros((8, 128, 512), dtype=np.float32)
    tri = np.triu(np.ones((128, 128), dtype=np.float32))
    for r in range(8):
        for i in range(4):
            d = 2 * i + half
            if r < d:
                pt[r][:, 128 * i:128 * (i + 1)] = 1.0
            elif r == d:
                pt[r][:, 128 * i:128 * (i + 1)] = tri
    return pt.astype(bfdt)


_BUILT = {}


def kernel(L0_attn_x, L0_attn_Wq, L0_attn_Wk, L0_attn_Wv, L0_attn_Wo,
           L0_attn_bo, L0_ffn_w_fc, L0_ffn_w_proj, _debug=False):
    x = np.asarray(L0_attn_x, dtype=np.float32)
    wqT = np.ascontiguousarray(np.asarray(L0_attn_Wq, np.float32).T)
    wkT = np.ascontiguousarray(np.asarray(L0_attn_Wk, np.float32).T)
    wvT = np.ascontiguousarray(np.asarray(L0_attn_Wv, np.float32).T)
    woT = np.ascontiguousarray(np.asarray(L0_attn_Wo, np.float32).T)
    wfcT = np.ascontiguousarray(np.asarray(L0_ffn_w_fc, np.float32).T)
    wpjT = np.ascontiguousarray(np.asarray(L0_ffn_w_proj, np.float32).T)
    bo_ = np.ascontiguousarray(np.asarray(L0_attn_bo, np.float32).reshape(1, E))
    ident = np.eye(128, dtype=np.float32).astype(bfdt)

    key = bool(_debug)
    if key not in _BUILT:
        _BUILT[key] = build_nc(debug=_debug)
    nc = _BUILT[key]

    in_maps = []
    for c in range(8):
        bidx, half = c // 2, c % 2
        rows = np.concatenate(
            [x[bidx, 128 * (2 * j + half):128 * (2 * j + half) + 128] for j in range(16)], axis=0)
        in_maps.append({
            "xf": np.ascontiguousarray(x[bidx]), "xq": np.ascontiguousarray(rows),
            "wqT": wqT, "wkT": wkT, "wvT": wvT, "woT": woT,
            "wfcT": wfcT, "wpjT": wpjT, "bo": bo_,
            "ptm": _make_masks(half), "idn": ident,
        })

    res = run_bass_kernel_spmd(nc, in_maps, core_ids=list(range(8)))

    outp = np.empty((B, S, E), dtype=np.float32)
    for c in range(8):
        bidx, half = c // 2, c % 2
        o = res.results[c]["out"]
        for j in range(16):
            outp[bidx, 128 * (2 * j + half):128 * (2 * j + half) + 128] = o[128 * j:128 * (j + 1)]
    if _debug:
        return outp, res
    return outp
